# version 18
# baseline (speedup 1.0000x reference)
import numpy as np
import ml_dtypes
import concourse.bass as bass
import concourse.mybir as mybir
from concourse.bass_utils import run_bass_kernel_spmd

F32 = mybir.dt.float32
BF16 = mybir.dt.bfloat16
AF = mybir.ActivationFunctionType
ALU = mybir.AluOpType
AX = mybir.AxisListType

NCORES = 8
T = 2048
D = 1024
DH = 512
H = 16
F = 32
G = 64
DF = 4096
C = 128
NCH = T // C
HD = 64
SQ = 8192
CPB = 4



class Buf:
    __slots__ = ("writer", "readers", "excl", "name")

    def __init__(self, name="", excl=False):
        self.writer = None
        self.readers = []
        self.excl = excl
        self.name = name


NLANES = 8


class Prog:
    ENGS = ("y", "g", "t", "v", "s")

    def __init__(self):
        self.q = {e: [] for e in self.ENGS}
        self.sems = (["t", "v", "s", "c"]
                     + [f"y{i}" for i in range(NLANES)]
                     + [f"g{i}" for i in range(NLANES)])
        self.cnt = {s: 0 for s in self.sems}
        self.ndma = {"y": 0, "g": 0}
        self.last_wait = {}

    def _need(self, eng, tok, acc=None):
        if tok is None:
            return
        sem, val = tok
        if sem == eng == "t":
            return
        if acc is not None:
            if val > acc.get(sem, 0):
                acc[sem] = val
            return
        key = (eng, sem)
        if self.last_wait.get(key, 0) >= val:
            return
        self.last_wait[key] = val
        self.q[eng].append(("w", sem, val))

    def emit(self, eng, fn, reads=(), writes=(), sem=None, step=None):
        if sem is None:
            if eng in ("y", "g"):
                sem = f"{eng}{self.ndma[eng] % NLANES}"
                self.ndma[eng] += 1
                step = 16
                if self.cnt[sem] > 0:
                    self._need(eng, (sem, self.cnt[sem]))
            else:
                sem = eng
                step = 1
        acc = {}
        for b in reads:
            self._need(eng, b.writer, acc)
            if b.excl:
                for r in b.readers:
                    self._need(eng, r, acc)
        for b in writes:
            self._need(eng, b.writer, acc)
            for r in b.readers:
                self._need(eng, r, acc)
        for s, v in acc.items():
            self._need(eng, (s, v))
        self.cnt[sem] += step
        tok = (sem, self.cnt[sem])
        self.q[eng].append(("i", fn, step, sem))
        for b in reads:
            if b.excl:
                b.readers = [tok]
            else:
                b.readers.append(tok)
        for b in writes:
            b.writer = tok
            b.readers = []
        return tok

    def barrier(self, skip=()):
        snap = dict(self.cnt)
        for e in self.ENGS:
            for s, v in snap.items():
                if s != e and v > 0 and not any(s.startswith(p) for p in skip):
                    self._need(e, (s, v))

    def flush(self, block, sems, nc=None):
        def mk(eng):
            def run(h):
                for item in self.q[eng]:
                    if item[0] == "w":
                        h.wait_ge(sems[item[1]], item[2])
                    else:
                        inst = item[1](h)
                        if inst is not None:
                            inst.then_inc(sems[item[3]], item[2])
            return run
        import contextlib
        lp = (nc.allow_low_precision(reason="bf16 operands/outputs deliberate; matmul accumulation stays fp32 PSUM")
              if nc is not None else contextlib.nullcontext())
        with lp:
            block.sync(mk("y"))
            block.gpsimd(mk("g"))
            block.tensor(mk("t"))
            block.vector(mk("v"))
            block.scalar(mk("s"))


class Tile:
    __slots__ = ("t", "b0", "nb", "dt", "esz", "buf", "parts")

    def __init__(self, t, b0, nb, dt, buf, parts=128):
        self.t, self.b0, self.nb, self.dt, self.buf, self.parts = t, b0, nb, dt, buf, parts
        self.esz = 2 if dt == BF16 else 4

    def ap(self, ps=None, cs=None):
        p0, p1 = (0, self.parts) if ps is None else ps
        n = self.nb // self.esz
        c0, c1 = (0, n) if cs is None else cs
        raw = self.t[p0:p1, self.b0 + c0 * self.esz: self.b0 + c1 * self.esz]
        return raw.bitcast(self.dt)

    @property
    def ncols(self):
        return self.nb // self.esz


class Arena:
    def __init__(self, nc, ctx, name, nbytes):
        self.t = ctx.enter_context(nc.sbuf_tensor(name, [128, nbytes], mybir.dt.uint8))
        self.off = 0
        self.size = nbytes
        self.stack = []
        self.peak = 0

    def alloc(self, dt, ncols, parts=128, name=""):
        esz = 2 if dt == BF16 else 4
        nb = ncols * esz
        off = (self.off + 63) & ~63
        assert off + nb <= self.size, f"arena overflow: {name} needs {nb}B at {off}, size {self.size}"
        self.off = off + nb
        self.peak = max(self.peak, self.off)
        return Tile(self.t, off, nb, dt, Buf(name), parts)

    def push(self):
        self.stack.append(self.off)

    def pop(self):
        self.off = self.stack.pop()


class Ctx:

    def __init__(self, nc, ctx, arena_bytes=200000):
        self.nc = nc
        self.ctx = ctx
        self.prog = Prog()
        self.ar = Arena(nc, ctx, "arena", arena_bytes)
        self.pw = []
        self.pb = []
        for i in range(4):
            t = ctx.enter_context(nc.psum_tensor(f"pw{i}", [128, 1024], F32))
            self.pw.append(t)
            self.pb.append((t, 0, Buf(f"pb{2*i}", excl=True)))
            self.pb.append((t, 512, Buf(f"pb{2*i+1}", excl=True)))

    def pbank(self, i, ps=None, cs=None):
        t, c0, buf = self.pb[i]
        p0, p1 = (0, 128) if ps is None else ps
        cc0, cc1 = (0, 512) if cs is None else cs
        return t[p0:p1, c0 + cc0: c0 + cc1], buf

    def pwide(self, i, ps=None, cs=None):
        t = self.pw[i]
        p0, p1 = (0, 128) if ps is None else ps
        c0, c1 = (0, 1024) if cs is None else cs
        return t[p0:p1, c0:c1], (self.pb[2 * i][2], self.pb[2 * i + 1][2])



def dma(cx, queue, dst_ap, src_ap, reads=(), writes=()):
    return cx.prog.emit(queue, lambda h, d=dst_ap, s=src_ap: h.dma_start(out=d, in_=s),
                        reads=reads, writes=writes)


def mm(cx, out_ap, lhsT_ap, rhs_ap, pbuf, reads, first, last, tp=None):
    def fn(h, o=out_ap, l=lhsT_ap, r=rhs_ap, st=first, sp=last, t=tp):
        return h.matmul(o, l, r, start=st, stop=sp, tile_position=t)
    return cx.prog.emit("t", fn, reads=list(reads), writes=[pbuf])


def act(cx, out_ap, in_ap, func, reads, writes, scale=None, bias=None, engine="s"):
    def fn(h, o=out_ap, i=in_ap, f=func, sc=scale, b=bias):
        kw = {}
        if sc is not None:
            kw["scale"] = sc
        if b is not None:
            kw["bias"] = b
        return h.activation(o, i, f, **kw)
    return cx.prog.emit(engine, fn, reads=list(reads), writes=list(writes))


def vop(cx, name, *args, reads=(), writes=(), engine="v", **kw):
    def fn(h, n=name, a=args, k=kw):
        return getattr(h, n)(*a, **k)
    return cx.prog.emit(engine, fn, reads=list(reads), writes=list(writes))


def memset(cx, ap, val, writes, engine="v"):
    return cx.prog.emit(engine, lambda h, a=ap, v=val: h.memset(a, v), writes=list(writes))


def recip_fast(cx, out_ap, in_ap, scratch_ap, reads, writes):
    def fn(h, o=out_ap, i=in_ap):
        return h.reciprocal(o, i)
    return cx.prog.emit("v", fn, reads=list(reads), writes=list(writes))



def emit_layernorm(cx, x_tiles, x_bufs, xn_t, dst_off=0, src_cs=None, wbuf=None,
                   pbase=0, scr=0):
    nc, pr, ar = cx.nc, cx.prog, cx.ar
    W = 512
    xsq = cx.ln_xsq2[scr]
    rows = cx.ln_rows2[scr]
    st_ap, st_buf = cx.pbank(pbase + 0)
    rb_ap, rb_buf = cx.pbank(pbase + 1)
    mb_ap, mb_buf = cx.pbank(pbase + 2)
    sum_ap = st_ap
    ones_col = cx.ones_col
    ones_row = cx.ones_row
    for dt in range(8):
        first = dt == 0
        last = dt == 7
        mm(cx, cx.pb[pbase][0][0:1, cx.pb[pbase][1]:cx.pb[pbase][1] + W],
           ones_col.ap(), x_tiles[dt], st_buf,
           [x_bufs[dt], ones_col.buf], first, last)
    for dt in range(8):
        act(cx, xsq.ap(cs=(0, W)), x_tiles[dt], AF.Square, [x_bufs[dt]], [xsq.buf])
        mm(cx, cx.pb[pbase][0][32:33, cx.pb[pbase][1]:cx.pb[pbase][1] + W],
           ones_col.ap(), xsq.ap(), st_buf,
           [xsq.buf, ones_col.buf], dt == 0, dt == 7)
    mean = rows.ap(ps=(0, 1), cs=(0, W))
    rstd = rows.ap(ps=(0, 1), cs=(W, 2 * W))
    mmr = rows.ap(ps=(0, 1), cs=(2 * W, 3 * W))
    scr = rows.ap(ps=(0, 1), cs=(3 * W, 4 * W))
    st0, _ = cx.pbank(pbase, ps=(0, 1), cs=(0, W))
    st1, _ = cx.pbank(pbase, ps=(32, 33), cs=(0, W))
    vop(cx, "tensor_scalar", mean, st0, 1.0 / D, None, ALU.mult, reads=[st_buf], writes=[rows.buf])
    vop(cx, "tensor_scalar", scr, st1, 1.0 / D, None, ALU.mult, reads=[st_buf], writes=[rows.buf])
    vop(cx, "tensor_tensor", rstd, mean, mean, ALU.mult, reads=[rows.buf], writes=[rows.buf])
    vop(cx, "tensor_tensor", scr, scr, rstd, ALU.subtract, reads=[rows.buf], writes=[rows.buf])
    act(cx, scr, scr, AF.Sqrt, [rows.buf], [rows.buf], bias=cx.eps_row.ap(ps=(0, 1), cs=(0, 1)))
    recip_fast(cx, rstd, scr, xsq.ap(ps=(0, 1)), [rows.buf], [rows.buf, xsq.buf])
    vop(cx, "tensor_tensor", mmr, mean, rstd, ALU.mult, reads=[rows.buf], writes=[rows.buf])
    rb_full, _ = cx.pbank(pbase + 1, cs=(0, W))
    mb_full, _ = cx.pbank(pbase + 2, cs=(0, W))
    mm(cx, rb_full, ones_row.ap(ps=(0, 1), cs=(0, 128)), rstd, rb_buf, [rows.buf, ones_row.buf], True, True)
    mm(cx, mb_full, ones_row.ap(ps=(0, 1), cs=(0, 128)), mmr, mb_buf, [rows.buf, ones_row.buf], True, True)
    tmp = xsq
    for dt in range(8):
        vop(cx, "tensor_tensor", tmp.ap(), x_tiles[dt], rb_full, ALU.mult,
            reads=[x_bufs[dt], rb_buf], writes=[tmp.buf])
        c0 = dt * T + dst_off
        vop(cx, "tensor_tensor", xn_t.ap(cs=(c0, c0 + W)), tmp.ap(), mb_full, ALU.subtract,
            reads=[tmp.buf, mb_buf], writes=[xn_t.buf if wbuf is None else wbuf])



def alloc_consts(cx):
    ar = cx.ar
    cx.ones_col = ar.alloc(F32, 1, name="ones_col")
    cx.ones_row = ar.alloc(F32, 128, parts=1, name="ones_row")
    cx.eps_row = ar.alloc(F32, 4, name="eps")
    cx.ones_bf = ar.alloc(BF16, 1, name="ones_bf")
    cx.ones_row_bf = ar.alloc(BF16, 128, parts=1, name="ones_row_bf")
    memset(cx, cx.ones_col.ap(), 1.0, [cx.ones_col.buf])
    memset(cx, cx.ones_row.ap(ps=(0, 1)), 1.0, [cx.ones_row.buf])
    memset(cx, cx.eps_row.ap(cs=(0, 1)), 1e-5, [cx.eps_row.buf])
    memset(cx, cx.ones_bf.ap(), 1.0, [cx.ones_bf.buf])
    memset(cx, cx.ones_row_bf.ap(ps=(0, 1)), 1.0, [cx.ones_row_bf.buf])


def load_wT_otile(cx, dst_tile, src_dram, src_buf, ot, kdim, pref="w"):
    nk = kdim // 128
    src = src_dram[:, ot * 128:(ot + 1) * 128].rearrange("(kt p) o -> p kt o", p=128)
    dst = dst_tile.ap().rearrange("p (kt o) -> p kt o", kt=nk)
    return dma(cx, "g", dst, src, reads=[src_buf], writes=[dst_tile.buf])



def _emit_l1(cx, io, iob, debug=False):
    nc, pr = cx.nc, cx.prog
    alloc_consts(cx)
    ar = cx.ar
    xn = ar.alloc(BF16, 8 * T, name="xn")
    xnb = [Buf(f"xn{c}") for c in range(4)]

    ar.push()
    wq_sb = ar.alloc(BF16, 8 * DH, name="wq_sb")
    wk_sb = ar.alloc(BF16, 8 * DH, name="wk_sb")
    wv_sb = ar.alloc(BF16, 8 * D, name="wv_sb")
    bq_sb = ar.alloc(F32, 4, name="bq_sb")
    bk_sb = ar.alloc(F32, 4, name="bk_sb")
    bvrow = ar.alloc(F32, D, parts=1, name="bv_row")
    bv_b = ar.alloc(F32, D, name="bv_b")
    m_sb = ar.alloc(F32, 4 * (G + 2), name="m_sb")
    memset(cx, m_sb.ap(), 0.0, [m_sb.buf])
    dma(cx, "g", wq_sb.ap().rearrange("p (dt o) -> p dt o", dt=8),
        io["wq"].ap().rearrange("(dt p) o -> p dt o", p=128),
        reads=[iob["wq"]], writes=[wq_sb.buf])
    dma(cx, "g", wk_sb.ap().rearrange("p (dt o) -> p dt o", dt=8),
        io["wk"].ap().rearrange("(dt p) o -> p dt o", p=128),
        reads=[iob["wk"]], writes=[wk_sb.buf])
    dma(cx, "g", wv_sb.ap().rearrange("p (dt o) -> p dt o", dt=8),
        io["wv"].ap().rearrange("(dt p) o -> p dt o", p=128),
        reads=[iob["wv"]], writes=[wv_sb.buf])
    dma(cx, "y", bq_sb.ap(), io["bq"][:, :], writes=[bq_sb.buf])
    dma(cx, "y", bk_sb.ap(), io["bk"][:, :], writes=[bk_sb.buf])
    dma(cx, "y", bvrow.ap(ps=(0, 1)), io["bv"][:, :], writes=[bvrow.buf])
    pc = {}
    for nm_s, nm_d, n_ot, nk in (("w1t", "w1b", 32, 8), ("w2t", "w2b", 32, 8),
                                 ("h1t", "h1b", 32, 32)):
        bl = []
        for ot in range(n_ot):
            b = Buf(f"{nm_d}{ot}")
            srcap = io[nm_s][:, ot * 128:(ot + 1) * 128].rearrange("(kt p) o -> p kt o", p=128)
            dma(cx, "g", io[nm_d][ot].rearrange("p (kt o) -> p kt o", kt=nk), srcap,
                reads=[iob[nm_s]], writes=[b])
            bl.append(b)
        pc[nm_d] = bl
    cx.pc = pc
    for half in range(2):
        bp, bpb = cx.pbank(3, cs=(0, 512))
        mm(cx, bp, cx.ones_row.ap(ps=(0, 1), cs=(0, 128)),
           bvrow.ap(ps=(0, 1), cs=(half * 512, (half + 1) * 512)), bpb,
           [bvrow.buf, cx.ones_row.buf], True, True)
        vop(cx, "tensor_copy", bv_b.ap(cs=(half * 512, (half + 1) * 512)), bp,
            reads=[bpb], writes=[bv_b.buf])

    cx.ln_xsq2 = [ar.alloc(F32, 512, name=f"ln_xsq{i}") for i in range(2)]
    cx.ln_rows2 = [ar.alloc(F32, 4 * 512, parts=1, name=f"ln_rows{i}") for i in range(2)]
    xch2 = [[ar.alloc(F32, 512, name=f"xch{j}_{i}") for i in range(8)] for j in range(2)]
    for ch in range(4):
        xch = xch2[ch % 2]
        for dt in range(8):
            dma(cx, "y", xch[dt].ap(), io["xT"][dt * 128:(dt + 1) * 128, ch * 512:(ch + 1) * 512],
                reads=[iob["xT"]], writes=[xch[dt].buf])
        emit_layernorm(cx, [t.ap() for t in xch], [t.buf for t in xch], xn, dst_off=ch * 512,
                       wbuf=xnb[ch], pbase=(0 if ch % 2 == 0 else 3), scr=ch % 2)
    if debug:
        for dt in range(8):
            dma(cx, "y", io["xn_dbg"][dt * 128:(dt + 1) * 128, :], xn.ap(cs=(dt * T, (dt + 1) * T)),
                reads=xnb, writes=[iob["xn_dbg"]])

    qch = ar.alloc(BF16, 4 * 512, name="qch")
    kch = ar.alloc(BF16, 4 * 512, name="kch")
    knt = ar.alloc(BF16, DH, name="knt")
    vnt = ar.alloc(BF16, H * (G + 1), name="vnt")
    ph0 = ar.alloc(F32, 512, name="phi0")
    ph1 = ar.alloc(F32, 512, name="phi1")

    def phi_from_psum(ps_ap, ps_buf, bias_col, bias_buf, out_ap, out_buf):
        vop(cx, "tensor_scalar", ph0.ap(), ps_ap, bias_col, None, ALU.add,
            reads=[ps_buf] + ([bias_buf] if bias_buf else []), writes=[ph0.buf])
        vop(cx, "tensor_scalar", ph1.ap(), ph0.ap(), 0.0, None, ALU.min,
            reads=[ph0.buf], writes=[ph1.buf])
        act(cx, ph1.ap(), ph1.ap(), AF.Exp, [ph1.buf], [ph1.buf])
        vop(cx, "tensor_scalar", ph0.ap(), ph0.ap(), 1.0, None, ALU.add,
            reads=[ph0.buf], writes=[ph0.buf])
        vop(cx, "tensor_tensor", out_ap, ph0.ap(), ph1.ap(), ALU.max,
            reads=[ph0.buf, ph1.buf], writes=[out_buf])

    for ch in range(4):
        cs0 = ch * 512
        for which, wsb, bcol, dsttile, ddram, dbuf in (
                (0, wq_sb, bq_sb, qch, io["qT"], iob["qT"]),
                (1, wk_sb, bk_sb, kch, io["kT"], iob["kT"])):
            for ot in range(4):
                pa, pbuf = cx.pbank(0 if ot % 2 == 0 else 1, cs=(0, 512))
                for kt in range(8):
                    mm(cx, pa, wsb.ap(cs=(kt * DH + ot * 128, kt * DH + (ot + 1) * 128)),
                       xn.ap(cs=(kt * T + cs0, kt * T + cs0 + 512)), pbuf,
                       [wsb.buf, xn.buf], kt == 0, kt == 7)
                phi_from_psum(pa, pbuf, bcol.ap(cs=(ot, ot + 1)), bcol.buf,
                              dsttile.ap(cs=(ot * 512, (ot + 1) * 512)), dsttile.buf)
            for ot in range(4):
                dma(cx, "y", ddram[ot * 128:(ot + 1) * 128, cs0:cs0 + 512],
                    dsttile.ap(cs=(ot * 512, (ot + 1) * 512)), reads=[dsttile.buf], writes=[dbuf])
        for tt in range(4):
            ti = ch * 4 + tt
            pa, pbuf = cx.pbank(2, cs=(0, 512))
            for kt in range(8):
                mm(cx, pa, xn.ap(cs=(kt * T + ti * 128, kt * T + (ti + 1) * 128)),
                   wk_sb.ap(cs=(kt * DH, (kt + 1) * DH)), pbuf,
                   [wk_sb.buf, xnb[ti // 4]], kt == 0, kt == 7)
            if ch == 0 and tt == 0:
                bkrow = ar.alloc(F32, DH, parts=1, name="bk_row")
                bk_b = ar.alloc(F32, DH, name="bk_b")
                dma(cx, "y", bkrow.ap(ps=(0, 1)), io["bk_row"][:, :], writes=[bkrow.buf])
                bp, bpb = cx.pbank(3, cs=(0, 512))
                mm(cx, bp, cx.ones_row.ap(ps=(0, 1), cs=(0, 128)), bkrow.ap(ps=(0, 1)), bpb,
                   [bkrow.buf, cx.ones_row.buf], True, True)
                vop(cx, "tensor_copy", bk_b.ap(), bp, reads=[bpb], writes=[bk_b.buf])
                cx.bk_b = bk_b
            vop(cx, "tensor_tensor", ph0.ap(), pa, cx.bk_b.ap(), ALU.add,
                reads=[pbuf, cx.bk_b.buf], writes=[ph0.buf])
            vop(cx, "tensor_scalar", ph1.ap(), ph0.ap(), 0.0, None, ALU.min,
                reads=[ph0.buf], writes=[ph1.buf])
            act(cx, ph1.ap(), ph1.ap(), AF.Exp, [ph1.buf], [ph1.buf])
            vop(cx, "tensor_scalar", ph0.ap(), ph0.ap(), 1.0, None, ALU.add,
                reads=[ph0.buf], writes=[ph0.buf])
            vop(cx, "tensor_tensor", knt.ap(), ph0.ap(), ph1.ap(), ALU.max,
                reads=[ph0.buf, ph1.buf], writes=[knt.buf])
            dma(cx, "y", io["kN"][ti * 128:(ti + 1) * 128, :], knt.ap(), reads=[knt.buf],
                writes=[iob["kN"]])
            pv, pvb = cx.pwide(3)
            for kt in range(8):
                for half in range(2):
                    mm(cx, cx.pw[3][:, half * 512:(half + 1) * 512],
                       xn.ap(cs=(kt * T + ti * 128, kt * T + (ti + 1) * 128)),
                       wv_sb.ap(cs=(kt * D + half * 512, kt * D + (half + 1) * 512)),
                       pvb[half], [wv_sb.buf, xnb[ti // 4]], kt == 0, kt == 7)
            memset(cx, vnt.ap().rearrange("p (h u) -> p h u", u=G + 1)[:, :, G:G + 1], 1.0,
                   [vnt.buf])
            vop(cx, "tensor_tensor",
                vnt.ap().rearrange("p (h u) -> p h u", u=G + 1)[:, :, 0:G],
                cx.pw[3][:, :].rearrange("p (h u) -> p h u", u=G),
                bv_b.ap().rearrange("p (h u) -> p h u", u=G),
                ALU.add, reads=[pvb[0], pvb[1], bv_b.buf], writes=[vnt.buf])
            dma(cx, "y", io["vN"][ti * 128:(ti + 1) * 128, :], vnt.ap(), reads=[vnt.buf],
                writes=[iob["vN"]])
            pm, pmb = cx.pbank(4, cs=(0, 4 * (G + 2)))
            for h in range(H):
                base = (h % 4) * 32
                cblk = (h // 4) * (G + 2)
                mm(cx, cx.pb[4][0][base:base + 32, cx.pb[4][1] + cblk:cx.pb[4][1] + cblk + G + 1],
                   knt.ap(cs=(h * F, (h + 1) * F)),
                   vnt.ap(cs=(h * (G + 1), (h + 1) * (G + 1))), pmb,
                   [knt.buf, vnt.buf], True, True, tp=(0, base))
            mv = m_sb.ap().rearrange("p (b u) -> p b u", u=G + 2)[:, :, 0:G + 1]
            pv = pm.rearrange("p (b u) -> p b u", u=G + 2)[:, :, 0:G + 1]
            vop(cx, "tensor_tensor", mv, mv, pv, ALU.add,
                reads=[m_sb.buf, pmb], writes=[m_sb.buf])
    dma(cx, "y", io["mtot"][:, :], m_sb.ap(), reads=[m_sb.buf], writes=[iob["mtot"]])
    ar.pop()
    pr.barrier(skip=("g",))
    for nm_s, nm_d, n_ot, nk in (("h2t", "h2b", 32, 32), ("w3t", "w3b", 8, 32)):
        bl = []
        for ot in range(n_ot):
            b = Buf(f"{nm_d}{ot}")
            srcap = io[nm_s][:, ot * 128:(ot + 1) * 128].rearrange("(kt p) o -> p kt o", p=128)
            dma(cx, "g", io[nm_d][ot].rearrange("p (kt o) -> p kt o", kt=nk), srcap,
                reads=[iob[nm_s]], writes=[b])
            bl.append(b)
        cx.pc[nm_d] = bl

    TTF = 1024
    ar.push()
    h_a = ar.alloc(BF16, 32 * TTF, name="h_a")
    h_b = ar.alloc(BF16, 32 * TTF, name="h_b")
    sil = ar.alloc(BF16, TTF, name="sil")
    wbufA = [ar.alloc(BF16, 8 * 128, name=f"wA{i}") for i in range(2)]
    wbufB = [ar.alloc(BF16, 8 * 128, name=f"wB{i}") for i in range(2)]
    wbufH = [ar.alloc(BF16, 32 * 128, name=f"wH{i}") for i in range(2)]
    s_sb = {}
    for nm, dr in (("w1s", io["w1s"]), ("w2s", io["w2s"]), ("h1s", io["h1s"]), ("h2s", io["h2s"]), ("w3s", io["w3s"])):
        t = ar.alloc(F32, dr.shape[1], name=nm)
        dma(cx, "y", t.ap(), dr[:, :], writes=[t.buf])
        s_sb[nm] = t
    xre = [ar.alloc(F32, TTF, name=f"xre{i}") for i in range(2)]
    pxo = [ar.alloc(F32, TTF, name=f"pxo{i}") for i in range(2)]

    for tt in range(2):
        tok0 = tt * TTF
        for ot in range(32):
            wa = wbufA[ot % 2]
            wb = wbufB[ot % 2]
            dma(cx, "y", wa.ap(), io["w1b"][ot], reads=[cx.pc["w1b"][ot]], writes=[wa.buf])
            dma(cx, "y", wb.ap(), io["w2b"][ot], reads=[cx.pc["w2b"][ot]], writes=[wb.buf])
            pA, pAb = cx.pwide(0 if ot % 2 == 0 else 1)
            pB, pBb = cx.pwide(2 if ot % 2 == 0 else 3)
            for kt in range(8):
                for half in range(2):
                    cs = (half * 512, (half + 1) * 512)
                    xb_ = xnb[(tok0 + half * 512) // 512]
                    mm(cx, cx.pw[(0 if ot % 2 == 0 else 1)][:, cs[0]:cs[1]],
                       wa.ap(cs=(kt * 128, (kt + 1) * 128)),
                       xn.ap(cs=(kt * T + tok0 + half * 512, kt * T + tok0 + (half + 1) * 512)),
                       pAb[half], [wa.buf, xb_], kt == 0, kt == 7)
                    mm(cx, cx.pw[(2 if ot % 2 == 0 else 3)][:, cs[0]:cs[1]],
                       wb.ap(cs=(kt * 128, (kt + 1) * 128)),
                       xn.ap(cs=(kt * T + tok0 + half * 512, kt * T + tok0 + (half + 1) * 512)),
                       pBb[half], [wb.buf, xb_], kt == 0, kt == 7)
            act(cx, sil.ap(), pA, AF.Silu, [pAb[0], pAb[1], s_sb["w1s"].buf], [sil.buf],
                scale=s_sb["w1s"].ap(cs=(ot, ot + 1)))
            vop(cx, "scalar_tensor_tensor", h_a.ap(cs=(ot * TTF, (ot + 1) * TTF)),
                pB, s_sb["w2s"].ap(cs=(ot, ot + 1)), sil.ap(), ALU.mult, ALU.mult,
                reads=[pBb[0], pBb[1], s_sb["w2s"].buf, sil.buf], writes=[h_a.buf])
        for stage, (wnm, snm, src, dst) in enumerate((
                ("h1b", "h1s", h_a, h_b),
                ("h2b", "h2s", h_b, h_a))):
            for ot in range(32):
                wh = wbufH[ot % 2]
                dma(cx, "y", wh.ap(), io[wnm][ot], reads=[cx.pc[wnm][ot]], writes=[wh.buf])
                pA, pAb = cx.pwide(0 if ot % 2 == 0 else 1)
                for kt in range(32):
                    for half in range(2):
                        mm(cx, cx.pw[(0 if ot % 2 == 0 else 1)][:, half * 512:(half + 1) * 512],
                           wh.ap(cs=(kt * 128, (kt + 1) * 128)),
                           src.ap(cs=(kt * TTF + half * 512, kt * TTF + (half + 1) * 512)),
                           pAb[half], [wh.buf, src.buf], kt == 0, kt == 31)
                act(cx, dst.ap(cs=(ot * TTF, (ot + 1) * TTF)), pA, AF.Gelu,
                    [pAb[0], pAb[1], s_sb[snm].buf], [dst.buf],
                    scale=s_sb[snm].ap(cs=(ot, ot + 1)))
        for dt in range(8):
            xr = xre[dt % 2]
            po = pxo[dt % 2]
            dma(cx, "y", xr.ap(), io["xT"][dt * 128:(dt + 1) * 128, tok0:tok0 + TTF],
                reads=[iob["xT"]], writes=[xr.buf])
            pA, pAb = cx.pwide(0 if dt % 2 == 0 else 1)
            wh = wbufH[dt % 2]
            dma(cx, "y", wh.ap(), io["w3b"][dt], reads=[cx.pc["w3b"][dt]], writes=[wh.buf])
            for kt in range(32):
                for half in range(2):
                    mm(cx, cx.pw[(0 if dt % 2 == 0 else 1)][:, half * 512:(half + 1) * 512],
                       wh.ap(cs=(kt * 128, (kt + 1) * 128)),
                       h_a.ap(cs=(kt * TTF + half * 512, kt * TTF + (half + 1) * 512)),
                       pAb[half], [wh.buf, h_a.buf], kt == 0, kt == 31)
            vop(cx, "scalar_tensor_tensor", po.ap(), pA, s_sb["w3s"].ap(cs=(dt, dt + 1)),
                xr.ap(), ALU.mult, ALU.add,
                reads=[pAb[0], pAb[1], s_sb["w3s"].buf, xr.buf], writes=[po.buf])
            dma(cx, "y", io["px1"][dt * 128:(dt + 1) * 128, tok0:tok0 + TTF], po.ap(),
                reads=[po.buf], writes=[iob["px1"]])
    ar.pop()
    pr.barrier(skip=("g",))



L1_INS = [("xT", [D, T], "f"), ("wq", [D, DH], "f"), ("wk", [D, DH], "f"),
          ("wv", [D, D], "f"), ("bq", [128, 4], "f"), ("bk", [128, 4], "f"),
          ("bk_row", [1, DH], "f"), ("bv", [1, D], "f"),
          ("w1t", [D, DF], "f"), ("w2t", [D, DF], "f"), ("h1t", [DF, DF], "f"),
          ("h2t", [DF, DF], "f"), ("w3t", [DF, D], "f"),
          ("w1s", [128, DF // 128], "f"), ("w2s", [128, DF // 128], "f"),
          ("h1s", [128, DF // 128], "f"), ("h2s", [128, DF // 128], "f"),
          ("w3s", [128, D // 128], "f")]
L1_MID = [("px1", [D, T], "f"), ("qT", [DH, T], "b"), ("kT", [DH, T], "b"),
          ("kN", [T, DH], "b"), ("vN", [T, H * (G + 1)], "b"),
          ("mtot", [128, 4 * (G + 2)], "f")]
L1_PRECAST = [("w1b", [32, 128, 8 * 128], "b"), ("w2b", [32, 128, 8 * 128], "b"),
              ("h1b", [32, 128, 32 * 128], "b"), ("h2b", [32, 128, 32 * 128], "b"),
              ("w3b", [8, 128, 32 * 128], "b")]


def _declare(nc, specs, kind):
    out = {}
    for name, shape, dt in specs:
        dtt = F32 if dt == "f" else BF16
        if kind is None:
            out[name] = nc.dram_tensor(name, shape, dtt)
        else:
            out[name] = nc.dram_tensor(name, shape, dtt, kind=kind)
    return out


def build_launch1(debug=False):
    nc = bass.Bass("TRN2", target_bir_lowering=False, debug=False)
    import contextlib
    ctx = contextlib.ExitStack()
    cx = Ctx(nc, ctx, arena_bytes=211500)
    io = {}
    io.update(_declare(nc, L1_INS, "ExternalInput"))
    io.update(_declare(nc, L1_MID, "ExternalOutput"))
    io.update(_declare(nc, L1_PRECAST, None))
    io["xn_dbg"] = nc.dram_tensor("xn_dbg", [D, T], BF16, kind="ExternalOutput") if debug else None
    iob = {n: Buf(n) for n in list(io)}
    with ctx:
        _emit_l1(cx, io, iob, debug=debug)
        with nc.Block() as block, _sems(nc, ctx, cx.prog.sems) as sems:
            cx.prog.flush(block, sems, nc=nc)
    return nc


def _sems(nc, ctx, names):
    import contextlib

    @contextlib.contextmanager
    def gen():
        with contextlib.ExitStack() as es:
            yield {n: es.enter_context(nc.semaphore(f"sem_{n}")) for n in names}
    return gen()



def _emit_l2(cx, io, iob, init_state, debug=False, post_qkuv=None):
    nc, pr = cx.nc, cx.prog
    GS = G + 2
    alloc_consts(cx)
    ar = cx.ar
    attn0 = ar.alloc(BF16, 8 * T, name="attn0")
    attn0b = [Buf(f"attn0q{i}") for i in range(4)]
    x1qb = [Buf(f"x1q{i}") for i in range(4)]
    xn1b = [Buf(f"xn1q{i}") for i in range(4)]

    x1 = ar.alloc(F32, 8 * T, name="x1")
    ar.push()
    qT = ar.alloc(BF16, 4 * T, name="qT")
    kT = ar.alloc(BF16, 4 * T, name="kT")
    kN = ar.alloc(BF16, 16 * DH, name="kN")
    vN = ar.alloc(BF16, 16 * H * (G + 1), name="vN")
    tri = ar.alloc(F32, 128, name="tri")
    m_sb = ar.alloc(F32, 4 * GS, name="m_sb")
    m_bf = ar.alloc(BF16, 4 * GS, name="m_bf")
    mp = [ar.alloc(F32, 4 * GS, name=f"mp{i}") for i in range(2)]
    a_bf2 = [ar.alloc(BF16, 128, name=f"a_bf{i}") for i in range(2)]
    zrow2 = [ar.alloc(F32, 128, parts=1, name=f"zrow{i}") for i in range(2)]
    zscr2 = [ar.alloc(F32, 128, parts=1, name=f"zscr{i}") for i in range(2)]
    zb_sb2 = [ar.alloc(F32, 128, name=f"zb_sb{i}") for i in range(2)]
    for dt in range(4):
        dma(cx, "y", qT.ap(cs=(dt * T, (dt + 1) * T)), io["qT"][dt * 128:(dt + 1) * 128, :],
            reads=[iob["qT"]], writes=[qT.buf])
        dma(cx, "y", kT.ap(cs=(dt * T, (dt + 1) * T)), io["kT"][dt * 128:(dt + 1) * 128, :],
            reads=[iob["kT"]], writes=[kT.buf])
    for ti in range(16):
        dma(cx, "y", kN.ap(cs=(ti * DH, (ti + 1) * DH)), io["kN"][ti * 128:(ti + 1) * 128, :],
            reads=[iob["kN"]], writes=[kN.buf])
        dma(cx, "y", vN.ap(cs=(ti * H * (G + 1), (ti + 1) * H * (G + 1))),
            io["vN"][ti * 128:(ti + 1) * 128, :], reads=[iob["vN"]], writes=[vN.buf])
    dma(cx, "y", tri.ap(), io["tri"][:, :], reads=[iob["tri"]], writes=[tri.buf])
    init_state(cx, m_sb, mp)
    vop(cx, "tensor_copy", m_bf.ap(), m_sb.ap(), reads=[m_sb.buf], writes=[m_bf.buf])

    for c in range(NCH):
        c0 = c * C
        for h in range(H):
            base = (h % 4) * 32
            cb = (h // 4) * GS
            qtile = h // 4
            qsl = qT.ap(ps=(base, base + 32), cs=(qtile * T + c0, qtile * T + c0 + C))
            ksl = kT.ap(ps=(base, base + 32), cs=(qtile * T + c0, qtile * T + c0 + C))
            a_bf = a_bf2[h % 2]
            zrow = zrow2[h % 2]
            zb_sb = zb_sb2[h % 2]
            pa, pab = cx.pbank(0 if h % 2 == 0 else 1, cs=(0, 128))
            mm(cx, pa, ksl, qsl, pab, [kT.buf, qT.buf], True, True, tp=(base, 0))
            vop(cx, "tensor_tensor", a_bf.ap(), pa, tri.ap(), ALU.mult,
                reads=[pab, tri.buf], writes=[a_bf.buf])
            pt, ptb = cx.pbank(2 if h % 2 == 0 else 3, cs=(0, 128))
            pt65 = cx.pb[2 if h % 2 == 0 else 3]
            att_ap = pt65[0][0:65, pt65[1]:pt65[1] + 128]
            mm(cx, att_ap, vN.ap(cs=(c * H * (G + 1) + h * (G + 1),
                                     c * H * (G + 1) + (h + 1) * (G + 1))),
               a_bf.ap(), ptb, [vN.buf, a_bf.buf], True, False)
            mm(cx, att_ap, m_bf.ap(ps=(base, base + 32), cs=(cb, cb + G + 1)), qsl,
               ptb, [m_bf.buf, qT.buf], False, True, tp=(base, 0))
            zsrc = pt65[0][64:65, pt65[1]:pt65[1] + 128]
            vop(cx, "tensor_scalar", zrow.ap(ps=(0, 1)), zsrc, 1e-6, None, ALU.add,
                reads=[ptb], writes=[zrow.buf])
            zscr = zscr2[h % 2]
            recip_fast(cx, zrow.ap(ps=(0, 1)), zrow.ap(ps=(0, 1)), zscr.ap(ps=(0, 1)),
                       [zrow.buf], [zrow.buf, zscr.buf])
            zbank = 4 if h % 2 == 0 else 6
            pz, pzb = cx.pbank(zbank, cs=(0, 128))
            pz64 = cx.pb[zbank][0][0:64, cx.pb[zbank][1]:cx.pb[zbank][1] + 128]
            mm(cx, pz64, cx.ones_row.ap(ps=(0, 1), cs=(0, 64)), zrow.ap(ps=(0, 1)),
               pzb, [cx.ones_row.buf, zrow.buf], True, True)
            prog_eng = "s"
            cx.prog.emit(prog_eng, lambda hh, o=zb_sb.ap(ps=(0, 64)), i=pz64: hh.copy(o, i),
                         reads=[pzb], writes=[zb_sb.buf])
            gtile, gbase = h // 2, (h % 2) * 64
            strip = attn0.ap(ps=(gbase, gbase + 64), cs=(gtile * T + c0, gtile * T + c0 + C))
            vop(cx, "tensor_tensor", strip,
                pt65[0][0:64, pt65[1]:pt65[1] + 128], zb_sb.ap(ps=(0, 64)), ALU.mult,
                reads=[ptb, zb_sb.buf], writes=[attn0.buf])
            pd, pdb = cx.pbank(5, cs=(0, 4 * GS))
            mm(cx, cx.pb[5][0][base:base + 32, cx.pb[5][1] + cb:cx.pb[5][1] + cb + G + 1],
               kN.ap(cs=(c * DH + h * F, c * DH + (h + 1) * F)),
               vN.ap(cs=(c * H * (G + 1) + h * (G + 1),
                         c * H * (G + 1) + (h + 1) * (G + 1))),
               pdb, [kN.buf, vN.buf], True, True, tp=(0, base))
        mv = m_sb.ap().rearrange("p (b u) -> p b u", u=GS)[:, :, 0:G + 1]
        pv = cx.pbank(5, cs=(0, 4 * GS))[0].rearrange("p (b u) -> p b u", u=GS)[:, :, 0:G + 1]
        vop(cx, "tensor_tensor", mv, mv, pv, ALU.add,
            reads=[m_sb.buf, cx.pb[5][2]], writes=[m_sb.buf])
        vop(cx, "tensor_copy", m_bf.ap(), m_sb.ap(), reads=[m_sb.buf], writes=[m_bf.buf])
    if debug:
        for dt in range(8):
            dma(cx, "y", io["a0_dbg"][dt * 128:(dt + 1) * 128, :],
                attn0.ap(cs=(dt * T, (dt + 1) * T)), reads=attn0b, writes=[iob["a0_dbg"]])

    ar.push()
    wo_sb = ar.alloc(BF16, 8 * D, name="wo_sb")
    bo_sb = ar.alloc(F32, 8, name="bo_sb")
    pxb = [ar.alloc(F32, 512, name=f"pxb{i}") for i in range(2)]
    dma(cx, "g", wo_sb.ap().rearrange("p (gt o) -> p gt o", gt=8),
        io["wo"].ap().rearrange("(gt p) o -> p gt o", p=128), reads=[iob["wo"]], writes=[wo_sb.buf])
    dma(cx, "y", bo_sb.ap(), io["bo"][:, :], reads=[iob["bo"]], writes=[bo_sb.buf])
    for qc in range(4):
        cs0 = qc * 512
        for dt in range(8):
            xb = pxb[dt % 2]
            dma(cx, "y", xb.ap(), io["px1"][dt * 128:(dt + 1) * 128, cs0:cs0 + 512],
                reads=[iob["px1"]], writes=[xb.buf])
            pa, pab = cx.pbank(0 if dt % 2 == 0 else 1)
            for gt in range(8):
                mm(cx, pa, wo_sb.ap(cs=(gt * D + dt * 128, gt * D + (dt + 1) * 128)),
                   attn0.ap(cs=(gt * T + cs0, gt * T + cs0 + 512)), pab,
                   [wo_sb.buf, attn0b[qc]], gt == 0, gt == 7)
            vop(cx, "scalar_tensor_tensor", x1.ap(cs=(dt * T + cs0, dt * T + cs0 + 512)),
                pa, bo_sb.ap(cs=(dt, dt + 1)), xb.ap(), ALU.add, ALU.add,
                reads=[pab, bo_sb.buf, xb.buf], writes=[x1qb[qc]])
    ar.pop()
    ar.pop()
    pr.barrier(skip=("g",))
    if debug:
        for dt in range(8):
            dma(cx, "y", io["x1_dbg"][dt * 128:(dt + 1) * 128, :],
                x1.ap(cs=(dt * T, (dt + 1) * T)), reads=x1qb, writes=[iob["x1_dbg"]])
    xn1 = ar.alloc(BF16, 8 * T, name="xn1")

    ar.push()
    cx.ln_xsq2 = [ar.alloc(F32, 512, name=f"ln_xsq{i}") for i in range(2)]
    cx.ln_rows2 = [ar.alloc(F32, 4 * 512, parts=1, name=f"ln_rows{i}") for i in range(2)]
    for ch in range(4):
        tiles = [x1.ap(cs=(dt * T + ch * 512, dt * T + ch * 512 + 512)) for dt in range(8)]
        emit_layernorm(cx, tiles, [x1qb[ch]] * 8, xn1, dst_off=ch * 512, wbuf=xn1b[ch],
                       pbase=(0 if ch % 2 == 0 else 3), scr=ch % 2)
    pr.barrier()
    ar.pop()

    ar.push()
    wbuf = [ar.alloc(BF16, 8 * 128, name=f"wq{i}") for i in range(2)]
    strip = [ar.alloc(BF16, 512, name=f"qs{i}") for i in range(2)]
    for gi, (col0, dst, dbuf) in enumerate(
            ((0, io["qTf"], iob["qTf"]), (D, io["kTf"], iob["kTf"]), (3 * D, io["gTf"], iob["gTf"]))):
        for ot in range(8):
            wb = wbuf[ot % 2]
            src = io["wquv"][:, col0 + ot * 128: col0 + (ot + 1) * 128].rearrange(
                "(kt p) o -> p kt o", p=128)
            dma(cx, "g", wb.ap().rearrange("p (kt o) -> p kt o", kt=8), src,
                reads=[iob["wquv"]], writes=[wb.buf])
            for qc in range(4):
                st = strip[qc % 2]
                pa, pab = cx.pbank(qc % 2)
                for kt in range(8):
                    mm(cx, pa, wb.ap(cs=(kt * 128, (kt + 1) * 128)),
                       xn1.ap(cs=(kt * T + qc * 512, kt * T + qc * 512 + 512)),
                       pab, [wb.buf, xn1.buf], kt == 0, kt == 7)
                act(cx, st.ap(), pa, AF.Silu, [pab], [st.buf])
                dma(cx, "y", dst[ot * 128:(ot + 1) * 128, qc * 512:(qc + 1) * 512],
                    st.ap(), reads=[st.buf], writes=[dbuf])
    wv_sb = ar.alloc(BF16, 8 * D, name="wv_sb")
    vstrip = [ar.alloc(BF16, D, name=f"vs{i}") for i in range(2)]
    dma(cx, "g", wv_sb.ap().rearrange("p (kt o) -> p kt o", kt=8),
        io["wquv"][:, 2 * D:3 * D].rearrange("(kt p) o -> p kt o", p=128),
        reads=[iob["wquv"]], writes=[wv_sb.buf])
    for tt in range(16):
        vs = vstrip[tt % 2]
        pa, pab = cx.pwide(1 if tt % 2 == 0 else 0)
        pwi = 1 if tt % 2 == 0 else 0
        for kt in range(8):
            for half in range(2):
                mm(cx, cx.pw[pwi][:, half * 512:(half + 1) * 512],
                   xn1.ap(cs=(kt * T + tt * 128, kt * T + (tt + 1) * 128)),
                   wv_sb.ap(cs=(kt * D + half * 512, kt * D + (half + 1) * 512)),
                   pab[half], [wv_sb.buf, xn1b[tt // 4]], kt == 0, kt == 7)
        act(cx, vs.ap(), pa, AF.Silu, [pab[0], pab[1]], [vs.buf])
        dma(cx, "y", io["vNf"][tt * 128:(tt + 1) * 128, :], vs.ap(),
            reads=[vs.buf], writes=[iob["vNf"]])
    if post_qkuv is not None:
        post_qkuv()
    ar.pop()
    pr.barrier(skip=("g", "c"))

    ar.push()
    projT = ar.alloc(F32, 2 * T, name="projT")
    projn = ar.alloc(BF16, 2 * T, name="projn")
    psq = ar.alloc(F32, 512, name="psq")
    rows2 = ar.alloc(F32, 2 * T, parts=1, name="rows2")
    wpt_sb = ar.alloc(BF16, 8 * 256, name="wpt_sb")
    bp_sb = ar.alloc(F32, 2, name="bp_sb")
    sim_sb = ar.alloc(F32, 8, name="sim_sb")
    simsc = ar.alloc(BF16, 8, name="simsc")
    crow = ar.alloc(F32, 8, parts=1, name="crow")
    emat = ar.alloc(F32, 128, parts=4, name="emat")
    retT4 = ar.alloc(F32, T, parts=4, name="retT4")
    l_sb = ar.alloc(F32, 4, name="l_sb")
    e_sb = ar.alloc(F32, 4, name="e_sb")
    red1 = ar.alloc(F32, 4, name="red1")
    ret_f = ar.alloc(F32, 4, name="ret_f")
    ret_bf = ar.alloc(BF16, 4, name="ret_bf")
    eye = ar.alloc(F32, 128, name="eye")
    retb_sb = ar.alloc(BF16, 512, name="retb_sb")
    xds = ar.alloc(BF16, 4 * 512, name="xds")
    wd_sb = ar.alloc(BF16, 8 * 128, name="wd_sb")
    wu_sb = ar.alloc(BF16, D, name="wu_sb")
    wt_b = [ar.alloc(BF16, 8 * 128, name=f"wt{i}") for i in range(2)]
    x1bo = [ar.alloc(F32, 512, name=f"x1bo{i}") for i in range(2)]

    dma(cx, "g", wpt_sb.ap().rearrange("p (kt o) -> p kt o", kt=8),
        io["Wpt"].ap().rearrange("(kt p) o -> p kt o", p=128), reads=[iob["Wpt"]], writes=[wpt_sb.buf])
    dma(cx, "y", bp_sb.ap(), io["bp"][:, :], reads=[iob["bp"]], writes=[bp_sb.buf])
    dma(cx, "y", sim_sb.ap().rearrange("p (pt e) -> p pt e", pt=2),
        io["sim"].ap().rearrange("(pt p) e -> p pt e", p=128), reads=[iob["sim"]], writes=[sim_sb.buf])
    dma(cx, "y", crow.ap(ps=(0, 1), cs=(4, 5)), io["temp"][:, :], reads=[iob["temp"]], writes=[crow.buf])
    dma(cx, "y", emat.ap(ps=(0, 4)), io["emat"][:, :], reads=[iob["emat"]], writes=[emat.buf])
    dma(cx, "g", wd_sb.ap().rearrange("p (kt o) -> p kt o", kt=8),
        io["wdt"].ap().rearrange("(kt p) o -> p kt o", p=128), reads=[iob["wdt"]], writes=[wd_sb.buf])
    dma(cx, "g", wu_sb.ap(), io["wut"][:, :], reads=[iob["wut"]], writes=[wu_sb.buf])

    for pt_i in range(2):
        for qc in range(4):
            pa, pab = cx.pbank(qc % 2)
            for kt in range(8):
                mm(cx, pa, wpt_sb.ap(cs=(kt * 256 + pt_i * 128, kt * 256 + (pt_i + 1) * 128)),
                   xn1.ap(cs=(kt * T + qc * 512, kt * T + qc * 512 + 512)),
                   pab, [wpt_sb.buf, xn1b[qc]], kt == 0, kt == 7)
            vop(cx, "tensor_scalar", projT.ap(cs=(pt_i * T + qc * 512, pt_i * T + qc * 512 + 512)),
                pa, bp_sb.ap(cs=(pt_i, pt_i + 1)), None, ALU.add,
                reads=[pab, bp_sb.buf], writes=[projT.buf])
    for qc in range(4):
        ps_st, ps_stb = cx.pbank(2, cs=(0, 512))
        for pt_i in range(2):
            act(cx, psq.ap(), projT.ap(cs=(pt_i * T + qc * 512, pt_i * T + qc * 512 + 512)),
                AF.Square, [projT.buf], [psq.buf])
            mm(cx, ps_st[0:1, :], cx.ones_col.ap(), psq.ap(), ps_stb,
               [cx.ones_col.buf, psq.buf], pt_i == 0, pt_i == 1)
        nrm = rows2.ap(ps=(0, 1), cs=(T + qc * 512, T + qc * 512 + 512))
        act(cx, nrm, ps_st[0:1, :], AF.Sqrt, [ps_stb], [rows2.buf])
        pinv = rows2.ap(ps=(0, 1), cs=(qc * 512, qc * 512 + 512))
        vop(cx, "tensor_scalar", nrm, nrm, 1e-12, None, ALU.max,
            reads=[rows2.buf], writes=[rows2.buf])
        recip_fast(cx, pinv, nrm, psq.ap(ps=(0, 1)), [rows2.buf], [rows2.buf, psq.buf])
        pz, pzb = cx.pbank(3, cs=(0, 512))
        mm(cx, pz, cx.ones_row.ap(ps=(0, 1), cs=(0, 128)), pinv, pzb,
           [cx.ones_row.buf, rows2.buf], True, True)
        for pt_i in range(2):
            vop(cx, "tensor_tensor", projn.ap(cs=(pt_i * T + qc * 512, pt_i * T + qc * 512 + 512)),
                projT.ap(cs=(pt_i * T + qc * 512, pt_i * T + qc * 512 + 512)), pz, ALU.mult,
                reads=[projT.buf, pzb], writes=[projn.buf])
    ps_st, ps_stb = cx.pbank(2, cs=(0, 8))
    for pt_i in range(2):
        act(cx, psq.ap(cs=(0, 4)), sim_sb.ap(cs=(pt_i * 4, (pt_i + 1) * 4)), AF.Square,
            [sim_sb.buf], [psq.buf])
        mm(cx, ps_st[0:1, 0:4], cx.ones_col.ap(), psq.ap(cs=(0, 4)), ps_stb,
           [cx.ones_col.buf, psq.buf], pt_i == 0, pt_i == 1)
    cn = crow.ap(ps=(0, 1), cs=(0, 4))
    act(cx, cn, ps_st[0:1, 0:4], AF.Sqrt, [ps_stb], [crow.buf])
    vop(cx, "tensor_scalar", cn, cn, 1e-12, None, ALU.max, reads=[crow.buf], writes=[crow.buf])
    vop(cx, "reciprocal", cn, cn, reads=[crow.buf], writes=[crow.buf])
    tmp1 = crow.ap(ps=(0, 1), cs=(4, 5))
    vop(cx, "tensor_scalar", tmp1, tmp1, float(np.log(100.0)), None, ALU.min,
        reads=[crow.buf], writes=[crow.buf])
    act(cx, tmp1, tmp1, AF.Exp, [crow.buf], [crow.buf])
    vop(cx, "tensor_scalar", cn, cn, tmp1, None, ALU.mult, reads=[crow.buf], writes=[crow.buf])
    pz, pzb = cx.pbank(3, cs=(0, 4))
    mm(cx, pz, cx.ones_row.ap(ps=(0, 1), cs=(0, 128)), cn, pzb,
       [cx.ones_row.buf, crow.buf], True, True)
    for pt_i in range(2):
        vop(cx, "tensor_tensor", simsc.ap(cs=(pt_i * 4, (pt_i + 1) * 4)),
            sim_sb.ap(cs=(pt_i * 4, (pt_i + 1) * 4)), pz, ALU.mult,
            reads=[sim_sb.buf, pzb], writes=[simsc.buf])
    dma(cx, "y", eye.ap(), io["eye"][:, :], reads=[iob["eye"]], writes=[eye.buf])
    for tt in range(16):
        pa, pab = cx.pbank(tt % 2, cs=(0, 4))
        for pt_i in range(2):
            mm(cx, pa, projn.ap(cs=(pt_i * T + tt * 128, pt_i * T + (tt + 1) * 128)),
               simsc.ap(cs=(pt_i * 4, (pt_i + 1) * 4)), pab,
               [projn.buf, simsc.buf], pt_i == 0, pt_i == 1)
        vop(cx, "tensor_copy", l_sb.ap(), pa, reads=[pab], writes=[l_sb.buf])
        vop(cx, "tensor_reduce", red1.ap(cs=(0, 1)), l_sb.ap(), AX.X, ALU.max,
            reads=[l_sb.buf], writes=[red1.buf], negate=True)
        act(cx, e_sb.ap(), l_sb.ap(), AF.Exp, [l_sb.buf, red1.buf], [e_sb.buf],
            bias=red1.ap(cs=(0, 1)))
        vop(cx, "tensor_reduce", red1.ap(cs=(1, 2)), e_sb.ap(), AX.X, ALU.add,
            reads=[e_sb.buf], writes=[red1.buf])
        vop(cx, "reciprocal", red1.ap(cs=(2, 3)), red1.ap(cs=(1, 2)),
            reads=[red1.buf], writes=[red1.buf])
        vop(cx, "tensor_scalar", ret_f.ap(), e_sb.ap(), red1.ap(cs=(2, 3)), None, ALU.mult,
            reads=[e_sb.buf, red1.buf], writes=[ret_f.buf])
        vop(cx, "tensor_reduce", red1.ap(cs=(3, 4)), ret_f.ap(), AX.X, ALU.max,
            reads=[ret_f.buf], writes=[red1.buf])
        vop(cx, "tensor_scalar", e_sb.ap(), ret_f.ap(), red1.ap(cs=(3, 4)), None, ALU.is_ge,
            reads=[ret_f.buf, red1.buf], writes=[e_sb.buf])
        vop(cx, "tensor_tensor", ret_f.ap(), ret_f.ap(), e_sb.ap(), ALU.mult,
            reads=[ret_f.buf, e_sb.buf], writes=[ret_f.buf])
        pb_t, pb_tb = cx.pbank(4, cs=(0, 128))
        def tr(hh, o=cx.pb[4][0][0:4, cx.pb[4][1]:cx.pb[4][1] + 128], i=ret_f.ap(), e=eye.ap()):
            return hh.transpose(o, i, e)
        cx.prog.emit("t", tr, reads=[ret_f.buf, eye.buf], writes=[pb_tb])
        vop(cx, "tensor_copy", retT4.ap(ps=(0, 4), cs=(tt * 128, (tt + 1) * 128)),
            cx.pb[4][0][0:4, cx.pb[4][1]:cx.pb[4][1] + 128], reads=[pb_tb], writes=[retT4.buf])
    for qc in range(4):
        prb, prbb = cx.pbank(4, cs=(0, 512))
        mm(cx, prb, emat.ap(ps=(0, 4)), retT4.ap(ps=(0, 4), cs=(qc * 512, qc * 512 + 512)),
           prbb, [emat.buf, retT4.buf], True, True)
        vop(cx, "tensor_copy", retb_sb.ap(), prb, reads=[prbb], writes=[retb_sb.buf])
        pd, pdb = cx.pbank(5, cs=(0, 512))
        for kt in range(8):
            mm(cx, pd, wd_sb.ap(cs=(kt * 128, (kt + 1) * 128)),
               xn1.ap(cs=(kt * T + qc * 512, kt * T + qc * 512 + 512)),
               pdb, [wd_sb.buf, xn1.buf], kt == 0, kt == 7)
        vop(cx, "tensor_tensor", xds.ap(cs=(qc * 512, qc * 512 + 512)), pd, retb_sb.ap(),
            ALU.mult, reads=[pdb, retb_sb.buf], writes=[xds.buf])
    for dt in range(8):
        wb = wt_b[dt % 2]
        dma(cx, "g", wb.ap().rearrange("p (kt o) -> p kt o", kt=8),
            io["Wt"][:, dt * 128:(dt + 1) * 128].rearrange("(kt p) o -> p kt o", p=128),
            reads=[iob["Wt"]], writes=[wb.buf])
        for qc in range(4):
            xo = x1bo[qc % 2]
            pa, pab = cx.pbank(qc % 2)
            for kt in range(8):
                mm(cx, pa, wb.ap(cs=(kt * 128, (kt + 1) * 128)),
                   xn1.ap(cs=(kt * T + qc * 512, kt * T + qc * 512 + 512)),
                   pab, [wb.buf, xn1.buf], kt == 0, False)
            mm(cx, pa, wu_sb.ap(cs=(dt * 128, (dt + 1) * 128)),
               xds.ap(cs=(qc * 512, qc * 512 + 512)), pab,
               [wu_sb.buf, xds.buf], False, True)
            vop(cx, "tensor_tensor", xo.ap(), pa,
                x1.ap(cs=(dt * T + qc * 512, dt * T + qc * 512 + 512)), ALU.add,
                reads=[pab, x1.buf], writes=[xo.buf])
            dma(cx, "y", io["x1b"][dt * 128:(dt + 1) * 128, qc * 512:(qc + 1) * 512],
                xo.ap(), reads=[xo.buf], writes=[iob["x1b"]])
    ar.pop()
    pr.barrier()



L2_INS = [("tri", [128, 128], "f"), ("wo", [D, D], "f"), ("bo", [128, 8], "f"),
          ("wquv", [D, 4 * D], "f"), ("Wt", [D, D], "f"), ("Wpt", [D, 256], "f"),
          ("bp", [128, 2], "f"), ("sim", [256, 4], "f"), ("temp", [1, 1], "f"),
          ("wdt", [D, 128], "f"), ("wut", [128, D], "f"), ("emat", [4, 128], "f"),
          ("eye", [128, 128], "f")]
L2_MID = [("x1b", [D, T], "f"), ("qTf", [D, T], "b"), ("gTf", [D, T], "b"),
          ("kTf", [D, T], "b"), ("vNf", [T, D], "b")]


def build_launch2(debug=False):
    nc = bass.Bass("TRN2", target_bir_lowering=False, debug=False)
    import contextlib
    ctx = contextlib.ExitStack()
    cx = Ctx(nc, ctx, arena_bytes=211500)
    io = {}
    io.update(_declare(nc, L1_MID, "ExternalInput"))
    io.update(_declare(nc, L2_INS, "ExternalInput"))
    io["mpre"] = nc.dram_tensor("mpre", [3, 128, 4 * (G + 2)], F32, kind="ExternalInput")
    io.update(_declare(nc, L2_MID, "ExternalOutput"))
    io["a0_dbg"] = nc.dram_tensor("a0_dbg", [D, T], BF16, kind="ExternalOutput") if debug else None
    io["x1_dbg"] = nc.dram_tensor("x1_dbg", [D, T], F32, kind="ExternalOutput") if debug else None
    iob = {n: Buf(n) for n in list(io)}

    def init_state(cx2, m_sb, mp):
        dma(cx2, "y", m_sb.ap(), io["mpre"][0, :, :], reads=[iob["mpre"]], writes=[m_sb.buf])
        dma(cx2, "y", mp[0].ap(), io["mpre"][1, :, :], reads=[iob["mpre"]], writes=[mp[0].buf])
        dma(cx2, "y", mp[1].ap(), io["mpre"][2, :, :], reads=[iob["mpre"]], writes=[mp[1].buf])
        vop(cx2, "tensor_tensor", m_sb.ap(), m_sb.ap(), mp[0].ap(), ALU.add,
            reads=[m_sb.buf, mp[0].buf], writes=[m_sb.buf])
        vop(cx2, "tensor_tensor", m_sb.ap(), m_sb.ap(), mp[1].ap(), ALU.add,
            reads=[m_sb.buf, mp[1].buf], writes=[m_sb.buf])

    with ctx:
        _emit_l2(cx, io, iob, init_state, debug=debug)
        with nc.Block() as block, _sems(nc, ctx, cx.prog.sems) as sems:
            cx.prog.flush(block, sems, nc=nc)
    return nc



def _emit_l3(cx, io, iob, debug=False):
    nc, pr = cx.nc, cx.prog
    NK = SQ // 128
    alloc_consts(cx)
    ar = cx.ar
    gated = ar.alloc(BF16, 8 * T, name="gated")
    ar.push()
    qT = ar.alloc(BF16, 8 * T, name="qT")
    gT = ar.alloc(BF16, 8 * T, name="gT")
    gm2 = ar.alloc(F32, 1, name="gm2")
    bt2 = ar.alloc(F32, 1, name="bt2")
    att_pair = ar.alloc(F32, 2048, name="att_pair")
    att_sqp = ar.alloc(BF16, 2048, name="att_sqp")
    ss_sb = [ar.alloc(F32, 2048, parts=1, name=f"ss{i}") for i in range(2)]
    rstd_bf = [ar.alloc(F32, 2048, parts=1, name=f"rstd{i}") for i in range(2)]
    attn_t = ar.alloc(F32, 512, name="attn_t")
    w_sb = [ar.alloc(BF16, 1024, name=f"w{i}") for i in range(2)]
    kTp = [ar.alloc(BF16, SQ, name=f"kTp{i}") for i in range(2)]
    vp = [[ar.alloc(BF16, NK * G, name=f"vp{i}{j}") for j in range(2)] for i in range(1)]
    vbar2 = [ar.alloc(F32, NK, name=f"vbar{i}") for i in range(2)]
    for dt in range(8):
        dma(cx, "y", qT.ap(cs=(dt * T, (dt + 1) * T)), io["qTf"][dt * 128:(dt + 1) * 128, :],
            reads=[iob["qTf"]], writes=[qT.buf])
        dma(cx, "y", gT.ap(cs=(dt * T, (dt + 1) * T)), io["gTf"][dt * 128:(dt + 1) * 128, :],
            reads=[iob["gTf"]], writes=[gT.buf])
    dma(cx, "y", gm2.ap(), io["gamma2"][:, :], reads=[iob["gamma2"]], writes=[gm2.buf])
    dma(cx, "y", bt2.ap(), io["beta2"][:, :], reads=[iob["beta2"]], writes=[bt2.buf])

    for pair in range(H // 2):
        kb = kTp[pair % 2]
        vbs = vp[0]
        for hh in range(2):
            h = pair * 2 + hh
            for r in range(CPB):
                dma(cx, "y", kb.ap(ps=(hh * 64, hh * 64 + 64), cs=(r * T, (r + 1) * T)),
                    io["kTa"][r * D + h * 64: r * D + (h + 1) * 64, :],
                    reads=[iob["kTa"]], writes=[kb.buf])
            dma(cx, "y", vbs[hh].ap().rearrange("p (n d) -> p n d", n=NK),
                io["vNa"][:, h * 64:(h + 1) * 64].rearrange("(n p) d -> p n d", p=128),
                reads=[iob["vNa"]], writes=[vbs[hh].buf])
        for hh in range(2):
            vb = vbs[hh]
            vbr = vbar2[hh]
            vop(cx, "tensor_reduce", vbr.ap(), vb.ap().rearrange("p (n d) -> p n d", n=NK),
                AX.X, ALU.add, reads=[vb.buf], writes=[vbr.buf])
            vop(cx, "tensor_scalar", vbr.ap(), vbr.ap(), 1.0 / G, None, ALU.mult,
                reads=[vbr.buf], writes=[vbr.buf])
            vb3 = vb.ap().rearrange("p (n d) -> p n d", n=NK)
            vba = vbr.ap()
            vbb = bass.AP(vba.tensor, vba.offset, [vba.ap[0], [1, NK], [0, G]])
            vop(cx, "tensor_tensor", vb3, vb3, vbb, ALU.subtract,
                reads=[vb.buf, vbr.buf], writes=[vb.buf])
        att0b, att1b = cx.pb[4], cx.pb[5]
        att0_ap = att0b[0][0:64, att0b[1]:att0b[1] + 512]
        att1_ap = att1b[0][64:128, att1b[1]:att1b[1] + 512]
        for qc in range(4):
            cs0 = qc * 512
            qsl0 = qT.ap(ps=(0, 64), cs=(pair * T + cs0, pair * T + cs0 + 512))
            qsl1 = qT.ap(ps=(64, 128), cs=(pair * T + cs0, pair * T + cs0 + 512))

            def emit_mm1(kt):
                pwi = kt % 2
                psc, pscb = cx.pwide(pwi)
                mm(cx, cx.pw[pwi][:, 0:512],
                   kb.ap(ps=(0, 64), cs=(kt * 128, (kt + 1) * 128)), qsl0,
                   pscb[0], [kb.buf, qT.buf], True, True, tp=(0, 0))
                mm(cx, cx.pw[pwi][:, 512:1024],
                   kb.ap(ps=(64, 128), cs=(kt * 128, (kt + 1) * 128)), qsl1,
                   pscb[1], [kb.buf, qT.buf], True, True, tp=(64, 0))

            emit_mm1(0)
            for kt in range(NK):
                psc, pscb = cx.pwide(kt % 2)
                ws = w_sb[kt % 2]
                act(cx, ws.ap(), psc, AF.Silu, [pscb[0], pscb[1]], [ws.buf], scale=0.125)
                if kt + 1 < NK:
                    emit_mm1(kt + 1)
                mm(cx, att0_ap, vbs[0].ap(cs=(kt * G, (kt + 1) * G)), ws.ap(cs=(0, 512)),
                   att0b[2], [vbs[0].buf, ws.buf], kt == 0, kt == NK - 1, tp=(0, 0))
                mm(cx, att1_ap, vbs[1].ap(cs=(kt * G, (kt + 1) * G)), ws.ap(cs=(512, 1024)),
                   att1b[2], [vbs[1].buf, ws.buf], kt == 0, kt == NK - 1, tp=(0, 64))
            vop(cx, "tensor_copy", att_pair.ap(ps=(0, 64), cs=(cs0, cs0 + 512)), att0_ap,
                reads=[att0b[2]], writes=[att_pair.buf])
            vop(cx, "tensor_copy", att_pair.ap(ps=(64, 128), cs=(cs0, cs0 + 512)), att1_ap,
                reads=[att1b[2]], writes=[att_pair.buf])
        vop(cx, "tensor_tensor", att_sqp.ap(), att_pair.ap(), att_pair.ap(), ALU.mult,
            reads=[att_pair.buf], writes=[att_sqp.buf])
        for hh in range(2):
            for qc in range(4):
                pss, pssb = cx.pbank(6, cs=(0, 512))
                mm(cx, pss[0:1, :], cx.ones_bf.ap(ps=(hh * 64, hh * 64 + 64)),
                   att_sqp.ap(ps=(hh * 64, hh * 64 + 64), cs=(qc * 512, qc * 512 + 512)),
                   pssb, [cx.ones_bf.buf, att_sqp.buf], True, True, tp=(hh * 64, 0))
                cx.prog.emit("s", lambda h2, o=ss_sb[hh].ap(ps=(0, 1), cs=(qc * 512, qc * 512 + 512)),
                             i=pss[0:1, :]: h2.copy(o, i),
                             reads=[pssb], writes=[ss_sb[hh].buf])
        for hh in range(2):
            act(cx, ss_sb[hh].ap(ps=(0, 1)), ss_sb[hh].ap(ps=(0, 1)), AF.Sqrt,
                [ss_sb[hh].buf], [ss_sb[hh].buf],
                scale=1.0 / G, bias=cx.eps_row.ap(ps=(0, 1), cs=(0, 1)))
            recip_fast(cx, rstd_bf[hh].ap(ps=(0, 1)), ss_sb[hh].ap(ps=(0, 1)),
                       rstd_bf[1 - hh].ap(ps=(0, 1)),
                       [ss_sb[hh].buf], [rstd_bf[hh].buf, rstd_bf[1 - hh].buf])
        for hh in range(2):
            base = hh * 64
            for qc in range(4):
                cs0 = qc * 512
                prs, prsb = cx.pbank(7, cs=(0, 512))
                prs64 = cx.pb[7][0][0:64, cx.pb[7][1]:cx.pb[7][1] + 512]
                mm(cx, prs64, cx.ones_row.ap(ps=(0, 1), cs=(0, 64)),
                   rstd_bf[hh].ap(ps=(0, 1), cs=(cs0, cs0 + 512)), prsb,
                   [cx.ones_row.buf, rstd_bf[hh].buf], True, True)
                vop(cx, "tensor_tensor", attn_t.ap(ps=(base, base + 64)),
                    att_pair.ap(ps=(base, base + 64), cs=(cs0, cs0 + 512)), prs64, ALU.mult,
                    reads=[att_pair.buf, prsb], writes=[attn_t.buf])
                vop(cx, "tensor_scalar", attn_t.ap(ps=(base, base + 64)),
                    attn_t.ap(ps=(base, base + 64)), gm2.ap(ps=(base, base + 64)),
                    bt2.ap(ps=(base, base + 64)), ALU.mult, ALU.add,
                    reads=[attn_t.buf, gm2.buf, bt2.buf], writes=[attn_t.buf])
                gsl = gated.ap(ps=(base, base + 64), cs=(pair * T + cs0, pair * T + cs0 + 512))
                vop(cx, "tensor_tensor", gsl, attn_t.ap(ps=(base, base + 64)),
                    gT.ap(ps=(base, base + 64), cs=(pair * T + cs0, pair * T + cs0 + 512)),
                    ALU.mult, reads=[attn_t.buf, gT.buf], writes=[gated.buf])
    pr.barrier()
    ar.pop()

    ar.push()
    wo_sb = ar.alloc(BF16, 8 * D, name="wo_sb")
    xb = [ar.alloc(F32, 512, name=f"xb{i}") for i in range(2)]
    oo = [ar.alloc(F32, 512, name=f"oo{i}") for i in range(2)]
    dma(cx, "g", wo_sb.ap().rearrange("p (ct o) -> p ct o", ct=8),
        io["wout"].ap().rearrange("(ct p) o -> p ct o", p=128), reads=[iob["wout"]], writes=[wo_sb.buf])
    for qc in range(4):
        cs0 = qc * 512
        for dt in range(8):
            x_t = xb[dt % 2]
            o_t = oo[dt % 2]
            dma(cx, "y", x_t.ap(), io["x1b"][dt * 128:(dt + 1) * 128, cs0:cs0 + 512],
                reads=[iob["x1b"]], writes=[x_t.buf])
            pa, pab = cx.pbank(0 if dt % 2 == 0 else 1)
            for ct in range(8):
                mm(cx, pa, wo_sb.ap(cs=(ct * D + dt * 128, ct * D + (dt + 1) * 128)),
                   gated.ap(cs=(ct * T + cs0, ct * T + cs0 + 512)), pab,
                   [wo_sb.buf, gated.buf], ct == 0, ct == 7)
            vop(cx, "tensor_tensor", o_t.ap(), pa, x_t.ap(), ALU.add,
                reads=[pab, x_t.buf], writes=[o_t.buf])
            dma(cx, "y", io["outT"][dt * 128:(dt + 1) * 128, cs0:cs0 + 512], o_t.ap(),
                reads=[o_t.buf], writes=[iob["outT"]])
    ar.pop()
    pr.barrier()



L3_INS = [("wout", [D, D], "f"), ("gamma2", [128, 1], "f"), ("beta2", [128, 1], "f")]


def build_launch3(debug=False):
    nc = bass.Bass("TRN2", target_bir_lowering=False, debug=False)
    import contextlib
    ctx = contextlib.ExitStack()
    cx = Ctx(nc, ctx, arena_bytes=211500)
    io = {}
    io.update(_declare(nc, [("x1b", [D, T], "f"), ("qTf", [D, T], "b"), ("gTf", [D, T], "b"),
                            ("kTa", [CPB * D, T], "b"), ("vNa", [SQ, D], "b")], "ExternalInput"))
    io.update(_declare(nc, L3_INS, "ExternalInput"))
    io.update(_declare(nc, [("outT", [D, T], "f")], "ExternalOutput"))
    iob = {n: Buf(n) for n in list(io)}
    with ctx:
        _emit_l3(cx, io, iob, debug=debug)
        with nc.Block() as block, _sems(nc, ctx, cx.prog.sems) as sems:
            cx.prog.flush(block, sems, nc=nc)
    return nc



def build_fused(debug=False):
    nc = bass.Bass("TRN2", target_bir_lowering=False, debug=False)
    import contextlib
    ctx = contextlib.ExitStack()
    cx = Ctx(nc, ctx, arena_bytes=211500)
    pr = cx.prog
    GS = G + 2
    io = {}
    io.update(_declare(nc, L1_INS, "ExternalInput"))
    io.update(_declare(nc, L1_MID, None))
    io.update(_declare(nc, L1_PRECAST, None))
    io.update(_declare(nc, L2_INS, "ExternalInput"))
    io.update(_declare(nc, L2_MID, None))
    io.update(_declare(nc, L3_INS, "ExternalInput"))
    io["mprem"] = nc.dram_tensor("mprem", [128, 8], F32, kind="ExternalInput")
    io["outT"] = nc.dram_tensor("outT", [D, T], F32, kind="ExternalOutput")
    io["mtot_ag"] = nc.dram_tensor("mtot_ag", [NCORES * 128, 4 * GS], F32)
    io["kTa"] = nc.dram_tensor("kTa", [CPB * D, T], BF16)
    io["vNa"] = nc.dram_tensor("vNa", [SQ, D], BF16)
    for n in ("xn_dbg", "a0_dbg", "x1_dbg"):
        io[n] = None
    iob = {n: Buf(n) for n in io}

    def init_state(cx2, m_sb, mp):
        ar = cx2.ar
        mask = ar.alloc(F32, 8, name="mprem")
        dma(cx2, "y", mask.ap(), io["mprem"][:, :], reads=[iob["mprem"]], writes=[mask.buf])
        memset(cx2, m_sb.ap(), 0.0, [m_sb.buf])
        for j in range(NCORES):
            mj = mp[j % 2]
            dma(cx2, "y", mj.ap(), io["mtot_ag"][j * 128:(j + 1) * 128, :],
                reads=[iob["mtot_ag"]], writes=[mj.buf])
            vop(cx2, "scalar_tensor_tensor", m_sb.ap(), mj.ap(), mask.ap(cs=(j, j + 1)),
                m_sb.ap(), ALU.mult, ALU.add,
                reads=[mj.buf, mask.buf, m_sb.buf], writes=[m_sb.buf])

    def cc(kind, rg, in_name, out_name):
        def fn(h, k=kind, r=rg, i=io[in_name], o=io[out_name]):
            return h.collective_compute("AllGather", mybir.AluOpType.bypass,
                                        replica_groups=r, ins=[i.ap().opt()],
                                        outs=[o.ap().opt()])
        pr.emit("g", fn, reads=[iob[in_name]], writes=[iob[out_name]], sem="c", step=1)

    with ctx:
        ar = cx.ar
        ar.push()
        _emit_l1(cx, io, iob, debug=False)
        ar.pop()
        pr.barrier()
        cc("AllGather", [list(range(NCORES))], "mtot", "mtot_ag")
        def post_qkuv():
            cc("AllGather", [[0, 1, 2, 3], [4, 5, 6, 7]], "kTf", "kTa")
            cc("AllGather", [[0, 1, 2, 3], [4, 5, 6, 7]], "vNf", "vNa")
        ar.push()
        _emit_l2(cx, io, iob, init_state, debug=False, post_qkuv=post_qkuv)
        ar.pop()
        pr.barrier(skip=("c",))
        ar.push()
        _emit_l3(cx, io, iob, debug=False)
        ar.pop()
        pr.barrier()
        with nc.Block() as block, _sems(nc, ctx, pr.sems) as sems:
            pr.flush(block, sems, nc=nc)
    return nc



def _prep_shared(inputs):
    f = np.ascontiguousarray
    sh = {}
    sh["wq"] = f(inputs["L0_attn_wq"].astype(np.float32))
    sh["wk"] = f(inputs["L0_attn_wk"].astype(np.float32))
    sh["wv"] = f(inputs["L0_attn_wv"].astype(np.float32))
    sh["bq"] = f(inputs["L0_attn_bq"].reshape(4, 128).T.astype(np.float32))
    sh["bk"] = f(inputs["L0_attn_bk"].reshape(4, 128).T.astype(np.float32))
    sh["bk_row"] = f(inputs["L0_attn_bk"].reshape(1, DH).astype(np.float32))
    sh["bv"] = f(inputs["L0_attn_bv"].reshape(1, D).astype(np.float32))
    sh["w1t"] = f(inputs["L0_ffn_w1_w"].T)
    sh["w2t"] = f(inputs["L0_ffn_w2_w"].T)
    sh["h1t"] = f(inputs["L0_ffn_h1_w"].T)
    sh["h2t"] = f(inputs["L0_ffn_h2_w"].T)
    sh["w3t"] = f(inputs["L0_ffn_w3_w"].T)
    for nm, key in (("w1s", "L0_ffn_w1_s"), ("w2s", "L0_ffn_w2_s"),
                    ("h1s", "L0_ffn_h1_s"), ("h2s", "L0_ffn_h2_s"), ("w3s", "L0_ffn_w3_s")):
        sh[nm] = f(inputs[key].reshape(-1, 128).T)
    return sh


def _shard_x(x):
    shards = []
    for c in range(NCORES):
        b, r = c // CPB, (c % CPB) * T
        shards.append(np.ascontiguousarray(x[b, r:r + T, :].T.astype(np.float32)))
    return shards


def run_launch1(inputs, debug=False):
    sh = _prep_shared(inputs)
    xs = _shard_x(np.asarray(inputs["L0_attn_x"]))
    nc = build_launch1(debug=debug)
    in_maps = []
    for c in range(NCORES):
        m = dict(sh)
        m["xT"] = xs[c]
        in_maps.append(m)
    res = run_bass_kernel_spmd(nc, in_maps, core_ids=list(range(NCORES)))
    return res.results


def _prep_l2(inputs):
    f = np.ascontiguousarray
    sh = {}
    sh["tri"] = np.triu(np.ones((128, 128), dtype=np.float32))
    sh["wo"] = f(inputs["L0_attn_wo"].astype(np.float32))
    sh["bo"] = f(inputs["L0_attn_bo"].reshape(8, 128).T.astype(np.float32))
    sh["wquv"] = f(inputs["L1_attn_w_qkuv"].astype(np.float32))
    sh["Wt"] = f(inputs["L1_ffn_W"].T.astype(np.float32))
    sh["Wpt"] = f(inputs["L1_ffn_Wp"].T.astype(np.float32))
    sh["bp"] = f(inputs["L1_ffn_bp"].reshape(2, 128).T.astype(np.float32))
    sh["sim"] = f(inputs["L1_ffn_sim"].astype(np.float32))
    sh["temp"] = f(inputs["L1_ffn_temp"].reshape(1, 1).astype(np.float32))
    wd = np.zeros((128, D), np.float32)
    wu = np.zeros((D, 128), np.float32)
    emat = np.zeros((4, 128), np.float32)
    bases = {3: (0, 8), 2: (32, 4), 1: (64, 2), 0: (96, 1)}
    for e, (b0, r) in bases.items():
        wd[b0:b0 + r, :] = inputs[f"L1_ffn_Wd{e}"]
        wu[:, b0:b0 + r] = inputs[f"L1_ffn_Wu{e}"]
        emat[e, b0:b0 + r] = 1.0
    sh["wdt"] = f(wd.T)
    sh["wut"] = f(wu.T)
    sh["emat"] = emat
    sh["eye"] = np.eye(128, dtype=np.float32)
    return sh


def run_launch2(inputs, l1res, debug=False):
    sh = _prep_l2(inputs)
    nc = build_launch2(debug=debug)
    zeros = np.zeros((128, 4 * 66), np.float32)
    in_maps = []
    for c in range(NCORES):
        m = dict(sh)
        for k in ("px1", "qT", "kT", "kN", "vN"):
            m[k] = np.ascontiguousarray(l1res[c][k])
        preds = [l1res[j]["mtot"] for j in range((c // CPB) * CPB, c)]
        while len(preds) < 3:
            preds.append(zeros)
        m["mpre"] = np.ascontiguousarray(np.stack(preds[:3], 0))
        in_maps.append(m)
    res = run_bass_kernel_spmd(nc, in_maps, core_ids=list(range(NCORES)))
    return res.results


def _prep_l3(inputs):
    f = np.ascontiguousarray
    sh = {}
    sh["wout"] = f(inputs["L1_attn_w_out"].astype(np.float32))
    sh["gamma2"] = f(np.tile(inputs["L1_attn_ln_gamma"].astype(np.float32), 2).reshape(128, 1))
    sh["beta2"] = f(np.tile(inputs["L1_attn_ln_beta"].astype(np.float32), 2).reshape(128, 1))
    return sh


def run_launch3(inputs, l2res, debug=False):
    sh = _prep_l3(inputs)
    nc = build_launch3(debug=debug)
    in_maps = []
    kta = [np.ascontiguousarray(np.concatenate([l2res[b * CPB + r]["kTf"] for r in range(CPB)], 0))
           for b in range(2)]
    vna = [np.ascontiguousarray(np.concatenate([l2res[b * CPB + r]["vNf"] for r in range(CPB)], 0))
           for b in range(2)]
    for c in range(NCORES):
        m = dict(sh)
        m["x1b"] = np.ascontiguousarray(l2res[c]["x1b"])
        m["qTf"] = np.ascontiguousarray(l2res[c]["qTf"])
        m["gTf"] = np.ascontiguousarray(l2res[c]["gTf"])
        m["kTa"] = kta[c // CPB]
        m["vNa"] = vna[c // CPB]
        in_maps.append(m)
    res = run_bass_kernel_spmd(nc, in_maps, core_ids=list(range(NCORES)))
    return res.results


def run_fused(inputs, debug=False):
    sh = {}
    sh.update(_prep_shared(inputs))
    sh.update(_prep_l2(inputs))
    sh.update(_prep_l3(inputs))
    xs = _shard_x(np.asarray(inputs["L0_attn_x"]))
    nc = build_fused(debug=debug)
    in_maps = []
    for c in range(NCORES):
        m = dict(sh)
        m["xT"] = xs[c]
        mask = np.zeros((128, NCORES), np.float32)
        g0 = (c // CPB) * CPB
        mask[:, g0:c] = 1.0
        m["mprem"] = mask
        in_maps.append(m)
    res = run_bass_kernel_spmd(nc, in_maps, core_ids=list(range(NCORES)))
    return res.results


FUSED = True


def kernel(**inputs):
    inputs = {k: np.asarray(v) for k, v in inputs.items()}
    if FUSED:
        res = run_fused(inputs)
        out = np.empty((2, SQ, D), np.float32)
        for c in range(NCORES):
            b, r = c // CPB, (c % CPB) * T
            out[b, r:r + T, :] = res[c]["outT"].T
        return out
    l1 = run_launch1(inputs)
    l2 = run_launch2(inputs, l1)
    l3 = run_launch3(inputs, l2)
    out = np.empty((2, SQ, D), np.float32)
    for c in range(NCORES):
        b, r = c // CPB, (c % CPB) * T
        out[b, r:r + T, :] = l3[c]["outT"].T
    return out


# version 19 (final)
# speedup vs baseline: 1.0188x; 1.0188x over previous
import numpy as np
import ml_dtypes
import concourse.bass as bass
import concourse.mybir as mybir
from concourse.bass_utils import run_bass_kernel_spmd

F32 = mybir.dt.float32
BF16 = mybir.dt.bfloat16
AF = mybir.ActivationFunctionType
ALU = mybir.AluOpType
AX = mybir.AxisListType

NCORES = 8
T = 2048
D = 1024
DH = 512
H = 16
F = 32
G = 64
DF = 4096
C = 128
NCH = T // C
HD = 64
SQ = 8192
CPB = 4



class Buf:
    __slots__ = ("writer", "readers", "excl", "name")

    def __init__(self, name="", excl=False):
        self.writer = None
        self.readers = []
        self.excl = excl
        self.name = name


NLANES = 8


class Prog:
    ENGS = ("y", "g", "t", "v", "s")

    def __init__(self):
        self.q = {e: [] for e in self.ENGS}
        self.sems = (["t", "v", "s", "c"]
                     + [f"y{i}" for i in range(NLANES)]
                     + [f"g{i}" for i in range(NLANES)])
        self.cnt = {s: 0 for s in self.sems}
        self.ndma = {"y": 0, "g": 0}
        self.last_wait = {}

    def _need(self, eng, tok, acc=None):
        if tok is None:
            return
        sem, val = tok
        if sem == eng == "t":
            return
        if acc is not None:
            if val > acc.get(sem, 0):
                acc[sem] = val
            return
        key = (eng, sem)
        if self.last_wait.get(key, 0) >= val:
            return
        self.last_wait[key] = val
        self.q[eng].append(("w", sem, val))

    def emit(self, eng, fn, reads=(), writes=(), sem=None, step=None):
        if sem is None:
            if eng in ("y", "g"):
                sem = f"{eng}{self.ndma[eng] % NLANES}"
                self.ndma[eng] += 1
                step = 16
                if self.cnt[sem] > 0:
                    self._need(eng, (sem, self.cnt[sem]))
            else:
                sem = eng
                step = 1
        acc = {}
        for b in reads:
            self._need(eng, b.writer, acc)
            if b.excl:
                for r in b.readers:
                    self._need(eng, r, acc)
        for b in writes:
            self._need(eng, b.writer, acc)
            for r in b.readers:
                self._need(eng, r, acc)
        for s, v in acc.items():
            self._need(eng, (s, v))
        self.cnt[sem] += step
        tok = (sem, self.cnt[sem])
        self.q[eng].append(("i", fn, step, sem))
        for b in reads:
            if b.excl:
                b.readers = [tok]
            else:
                b.readers.append(tok)
        for b in writes:
            b.writer = tok
            b.readers = []
        return tok

    def barrier(self, skip=()):
        snap = dict(self.cnt)
        for e in self.ENGS:
            for s, v in snap.items():
                if s != e and v > 0 and not any(s.startswith(p) for p in skip):
                    self._need(e, (s, v))

    def flush(self, block, sems, nc=None):
        def mk(eng):
            def run(h):
                for item in self.q[eng]:
                    if item[0] == "w":
                        h.wait_ge(sems[item[1]], item[2])
                    else:
                        inst = item[1](h)
                        if inst is not None:
                            inst.then_inc(sems[item[3]], item[2])
            return run
        import contextlib
        lp = (nc.allow_low_precision(reason="bf16 operands/outputs deliberate; matmul accumulation stays fp32 PSUM")
              if nc is not None else contextlib.nullcontext())
        with lp:
            block.sync(mk("y"))
            block.gpsimd(mk("g"))
            block.tensor(mk("t"))
            block.vector(mk("v"))
            block.scalar(mk("s"))


class Tile:
    __slots__ = ("t", "b0", "nb", "dt", "esz", "buf", "parts")

    def __init__(self, t, b0, nb, dt, buf, parts=128):
        self.t, self.b0, self.nb, self.dt, self.buf, self.parts = t, b0, nb, dt, buf, parts
        self.esz = 2 if dt == BF16 else 4

    def ap(self, ps=None, cs=None):
        p0, p1 = (0, self.parts) if ps is None else ps
        n = self.nb // self.esz
        c0, c1 = (0, n) if cs is None else cs
        raw = self.t[p0:p1, self.b0 + c0 * self.esz: self.b0 + c1 * self.esz]
        return raw.bitcast(self.dt)

    @property
    def ncols(self):
        return self.nb // self.esz


class Arena:
    def __init__(self, nc, ctx, name, nbytes):
        self.t = ctx.enter_context(nc.sbuf_tensor(name, [128, nbytes], mybir.dt.uint8))
        self.off = 0
        self.size = nbytes
        self.stack = []
        self.peak = 0

    def alloc(self, dt, ncols, parts=128, name=""):
        esz = 2 if dt == BF16 else 4
        nb = ncols * esz
        off = (self.off + 63) & ~63
        assert off + nb <= self.size, f"arena overflow: {name} needs {nb}B at {off}, size {self.size}"
        self.off = off + nb
        self.peak = max(self.peak, self.off)
        return Tile(self.t, off, nb, dt, Buf(name), parts)

    def push(self):
        self.stack.append(self.off)

    def pop(self):
        self.off = self.stack.pop()


class Ctx:

    def __init__(self, nc, ctx, arena_bytes=200000):
        self.nc = nc
        self.ctx = ctx
        self.prog = Prog()
        self.ar = Arena(nc, ctx, "arena", arena_bytes)
        self.pw = []
        self.pb = []
        for i in range(4):
            t = ctx.enter_context(nc.psum_tensor(f"pw{i}", [128, 1024], F32))
            self.pw.append(t)
            self.pb.append((t, 0, Buf(f"pb{2*i}", excl=True)))
            self.pb.append((t, 512, Buf(f"pb{2*i+1}", excl=True)))

    def pbank(self, i, ps=None, cs=None):
        t, c0, buf = self.pb[i]
        p0, p1 = (0, 128) if ps is None else ps
        cc0, cc1 = (0, 512) if cs is None else cs
        return t[p0:p1, c0 + cc0: c0 + cc1], buf

    def pwide(self, i, ps=None, cs=None):
        t = self.pw[i]
        p0, p1 = (0, 128) if ps is None else ps
        c0, c1 = (0, 1024) if cs is None else cs
        return t[p0:p1, c0:c1], (self.pb[2 * i][2], self.pb[2 * i + 1][2])



def dma(cx, queue, dst_ap, src_ap, reads=(), writes=()):
    return cx.prog.emit(queue, lambda h, d=dst_ap, s=src_ap: h.dma_start(out=d, in_=s),
                        reads=reads, writes=writes)


def mm(cx, out_ap, lhsT_ap, rhs_ap, pbuf, reads, first, last, tp=None):
    def fn(h, o=out_ap, l=lhsT_ap, r=rhs_ap, st=first, sp=last, t=tp):
        return h.matmul(o, l, r, start=st, stop=sp, tile_position=t)
    return cx.prog.emit("t", fn, reads=list(reads), writes=[pbuf])


def act(cx, out_ap, in_ap, func, reads, writes, scale=None, bias=None, engine="s"):
    def fn(h, o=out_ap, i=in_ap, f=func, sc=scale, b=bias):
        kw = {}
        if sc is not None:
            kw["scale"] = sc
        if b is not None:
            kw["bias"] = b
        return h.activation(o, i, f, **kw)
    return cx.prog.emit(engine, fn, reads=list(reads), writes=list(writes))


def vop(cx, name, *args, reads=(), writes=(), engine="v", **kw):
    def fn(h, n=name, a=args, k=kw):
        return getattr(h, n)(*a, **k)
    return cx.prog.emit(engine, fn, reads=list(reads), writes=list(writes))


def memset(cx, ap, val, writes, engine="v"):
    return cx.prog.emit(engine, lambda h, a=ap, v=val: h.memset(a, v), writes=list(writes))


def recip_fast(cx, out_ap, in_ap, scratch_ap, reads, writes):
    def fn(h, o=out_ap, i=in_ap):
        return h.reciprocal(o, i)
    return cx.prog.emit("v", fn, reads=list(reads), writes=list(writes))



def emit_layernorm(cx, x_tiles, x_bufs, xn_t, dst_off=0, src_cs=None, wbuf=None,
                   pbase=0, scr=0):
    nc, pr, ar = cx.nc, cx.prog, cx.ar
    W = 512
    xsq = cx.ln_xsq2[scr]
    rows = cx.ln_rows2[scr]
    st_ap, st_buf = cx.pbank(pbase + 0)
    rb_ap, rb_buf = cx.pbank(pbase + 1)
    mb_ap, mb_buf = cx.pbank(pbase + 2)
    sum_ap = st_ap
    ones_col = cx.ones_col
    ones_row = cx.ones_row
    for dt in range(8):
        first = dt == 0
        last = dt == 7
        mm(cx, cx.pb[pbase][0][0:1, cx.pb[pbase][1]:cx.pb[pbase][1] + W],
           ones_col.ap(), x_tiles[dt], st_buf,
           [x_bufs[dt], ones_col.buf], first, last)
    for dt in range(8):
        act(cx, xsq.ap(cs=(0, W)), x_tiles[dt], AF.Square, [x_bufs[dt]], [xsq.buf])
        mm(cx, cx.pb[pbase][0][32:33, cx.pb[pbase][1]:cx.pb[pbase][1] + W],
           ones_col.ap(), xsq.ap(), st_buf,
           [xsq.buf, ones_col.buf], dt == 0, dt == 7)
    mean = rows.ap(ps=(0, 1), cs=(0, W))
    rstd = rows.ap(ps=(0, 1), cs=(W, 2 * W))
    mmr = rows.ap(ps=(0, 1), cs=(2 * W, 3 * W))
    scr = rows.ap(ps=(0, 1), cs=(3 * W, 4 * W))
    st0, _ = cx.pbank(pbase, ps=(0, 1), cs=(0, W))
    st1, _ = cx.pbank(pbase, ps=(32, 33), cs=(0, W))
    vop(cx, "tensor_scalar", mean, st0, 1.0 / D, None, ALU.mult, reads=[st_buf], writes=[rows.buf])
    vop(cx, "tensor_scalar", scr, st1, 1.0 / D, None, ALU.mult, reads=[st_buf], writes=[rows.buf])
    vop(cx, "tensor_tensor", rstd, mean, mean, ALU.mult, reads=[rows.buf], writes=[rows.buf])
    vop(cx, "tensor_tensor", scr, scr, rstd, ALU.subtract, reads=[rows.buf], writes=[rows.buf])
    act(cx, scr, scr, AF.Sqrt, [rows.buf], [rows.buf], bias=cx.eps_row.ap(ps=(0, 1), cs=(0, 1)))
    recip_fast(cx, rstd, scr, xsq.ap(ps=(0, 1)), [rows.buf], [rows.buf, xsq.buf])
    vop(cx, "tensor_tensor", mmr, mean, rstd, ALU.mult, reads=[rows.buf], writes=[rows.buf])
    rb_full, _ = cx.pbank(pbase + 1, cs=(0, W))
    mb_full, _ = cx.pbank(pbase + 2, cs=(0, W))
    mm(cx, rb_full, ones_row.ap(ps=(0, 1), cs=(0, 128)), rstd, rb_buf, [rows.buf, ones_row.buf], True, True)
    mm(cx, mb_full, ones_row.ap(ps=(0, 1), cs=(0, 128)), mmr, mb_buf, [rows.buf, ones_row.buf], True, True)
    tmp = xsq
    for dt in range(8):
        vop(cx, "tensor_tensor", tmp.ap(), x_tiles[dt], rb_full, ALU.mult,
            reads=[x_bufs[dt], rb_buf], writes=[tmp.buf])
        c0 = dt * T + dst_off
        vop(cx, "tensor_tensor", xn_t.ap(cs=(c0, c0 + W)), tmp.ap(), mb_full, ALU.subtract,
            reads=[tmp.buf, mb_buf], writes=[xn_t.buf if wbuf is None else wbuf])



def alloc_consts(cx):
    ar = cx.ar
    cx.ones_col = ar.alloc(F32, 1, name="ones_col")
    cx.ones_row = ar.alloc(F32, 128, parts=1, name="ones_row")
    cx.eps_row = ar.alloc(F32, 4, name="eps")
    cx.ones_bf = ar.alloc(BF16, 1, name="ones_bf")
    cx.ones_row_bf = ar.alloc(BF16, 128, parts=1, name="ones_row_bf")
    memset(cx, cx.ones_col.ap(), 1.0, [cx.ones_col.buf])
    memset(cx, cx.ones_row.ap(ps=(0, 1)), 1.0, [cx.ones_row.buf])
    memset(cx, cx.eps_row.ap(cs=(0, 1)), 1e-5, [cx.eps_row.buf])
    memset(cx, cx.ones_bf.ap(), 1.0, [cx.ones_bf.buf])
    memset(cx, cx.ones_row_bf.ap(ps=(0, 1)), 1.0, [cx.ones_row_bf.buf])


def load_wT_otile(cx, dst_tile, src_dram, src_buf, ot, kdim, pref="w"):
    nk = kdim // 128
    src = src_dram[:, ot * 128:(ot + 1) * 128].rearrange("(kt p) o -> p kt o", p=128)
    dst = dst_tile.ap().rearrange("p (kt o) -> p kt o", kt=nk)
    return dma(cx, "g", dst, src, reads=[src_buf], writes=[dst_tile.buf])



def _emit_l1(cx, io, iob, debug=False):
    nc, pr = cx.nc, cx.prog
    alloc_consts(cx)
    ar = cx.ar
    xn = ar.alloc(BF16, 8 * T, name="xn")
    xnb = [Buf(f"xn{c}") for c in range(4)]

    ar.push()
    wq_sb = ar.alloc(BF16, 8 * DH, name="wq_sb")
    wk_sb = ar.alloc(BF16, 8 * DH, name="wk_sb")
    wv_sb = ar.alloc(BF16, 8 * D, name="wv_sb")
    bq_sb = ar.alloc(F32, 4, name="bq_sb")
    bk_sb = ar.alloc(F32, 4, name="bk_sb")
    bvrow = ar.alloc(F32, D, parts=1, name="bv_row")
    bv_b = ar.alloc(F32, D, name="bv_b")
    m_sb = ar.alloc(F32, 4 * (G + 2), name="m_sb")
    memset(cx, m_sb.ap(), 0.0, [m_sb.buf])
    dma(cx, "g", wq_sb.ap().rearrange("p (dt o) -> p dt o", dt=8),
        io["wq"].ap().rearrange("(dt p) o -> p dt o", p=128),
        reads=[iob["wq"]], writes=[wq_sb.buf])
    dma(cx, "g", wk_sb.ap().rearrange("p (dt o) -> p dt o", dt=8),
        io["wk"].ap().rearrange("(dt p) o -> p dt o", p=128),
        reads=[iob["wk"]], writes=[wk_sb.buf])
    dma(cx, "g", wv_sb.ap().rearrange("p (dt o) -> p dt o", dt=8),
        io["wv"].ap().rearrange("(dt p) o -> p dt o", p=128),
        reads=[iob["wv"]], writes=[wv_sb.buf])
    dma(cx, "y", bq_sb.ap(), io["bq"][:, :], writes=[bq_sb.buf])
    dma(cx, "y", bk_sb.ap(), io["bk"][:, :], writes=[bk_sb.buf])
    dma(cx, "y", bvrow.ap(ps=(0, 1)), io["bv"][:, :], writes=[bvrow.buf])
    pc = {}
    for nm_s, nm_d, n_ot, nk in (("w1t", "w1b", 32, 8), ("w2t", "w2b", 32, 8),
                                 ("h1t", "h1b", 32, 32)):
        bl = []
        for ot in range(n_ot):
            b = Buf(f"{nm_d}{ot}")
            srcap = io[nm_s][:, ot * 128:(ot + 1) * 128].rearrange("(kt p) o -> p kt o", p=128)
            dma(cx, "g", io[nm_d][ot].rearrange("p (kt o) -> p kt o", kt=nk), srcap,
                reads=[iob[nm_s]], writes=[b])
            bl.append(b)
        pc[nm_d] = bl
    cx.pc = pc
    for half in range(2):
        bp, bpb = cx.pbank(3, cs=(0, 512))
        mm(cx, bp, cx.ones_row.ap(ps=(0, 1), cs=(0, 128)),
           bvrow.ap(ps=(0, 1), cs=(half * 512, (half + 1) * 512)), bpb,
           [bvrow.buf, cx.ones_row.buf], True, True)
        vop(cx, "tensor_copy", bv_b.ap(cs=(half * 512, (half + 1) * 512)), bp,
            reads=[bpb], writes=[bv_b.buf])

    cx.ln_xsq2 = [ar.alloc(F32, 512, name=f"ln_xsq{i}") for i in range(2)]
    cx.ln_rows2 = [ar.alloc(F32, 4 * 512, parts=1, name=f"ln_rows{i}") for i in range(2)]
    xch2 = [[ar.alloc(F32, 512, name=f"xch{j}_{i}") for i in range(8)] for j in range(2)]
    for ch in range(4):
        xch = xch2[ch % 2]
        for dt in range(8):
            dma(cx, "y", xch[dt].ap(), io["xT"][dt * 128:(dt + 1) * 128, ch * 512:(ch + 1) * 512],
                reads=[iob["xT"]], writes=[xch[dt].buf])
        emit_layernorm(cx, [t.ap() for t in xch], [t.buf for t in xch], xn, dst_off=ch * 512,
                       wbuf=xnb[ch], pbase=(0 if ch % 2 == 0 else 3), scr=ch % 2)
    if debug:
        for dt in range(8):
            dma(cx, "y", io["xn_dbg"][dt * 128:(dt + 1) * 128, :], xn.ap(cs=(dt * T, (dt + 1) * T)),
                reads=xnb, writes=[iob["xn_dbg"]])

    qch = ar.alloc(BF16, 4 * 512, name="qch")
    kch = ar.alloc(BF16, 4 * 512, name="kch")
    knt = ar.alloc(BF16, DH, name="knt")
    vnt = ar.alloc(BF16, H * (G + 1), name="vnt")
    ph0 = ar.alloc(F32, 512, name="phi0")
    ph1 = ar.alloc(F32, 512, name="phi1")

    def phi_from_psum(ps_ap, ps_buf, bias_col, bias_buf, out_ap, out_buf):
        vop(cx, "tensor_scalar", ph0.ap(), ps_ap, bias_col, None, ALU.add,
            reads=[ps_buf] + ([bias_buf] if bias_buf else []), writes=[ph0.buf])
        vop(cx, "tensor_scalar", ph1.ap(), ph0.ap(), 0.0, None, ALU.min,
            reads=[ph0.buf], writes=[ph1.buf])
        act(cx, ph1.ap(), ph1.ap(), AF.Exp, [ph1.buf], [ph1.buf])
        vop(cx, "tensor_scalar", ph0.ap(), ph0.ap(), 1.0, None, ALU.add,
            reads=[ph0.buf], writes=[ph0.buf])
        vop(cx, "tensor_tensor", out_ap, ph0.ap(), ph1.ap(), ALU.max,
            reads=[ph0.buf, ph1.buf], writes=[out_buf])

    for ch in range(4):
        cs0 = ch * 512
        for which, wsb, bcol, dsttile, ddram, dbuf in (
                (0, wq_sb, bq_sb, qch, io["qT"], iob["qT"]),
                (1, wk_sb, bk_sb, kch, io["kT"], iob["kT"])):
            for ot in range(4):
                pa, pbuf = cx.pbank(0 if ot % 2 == 0 else 1, cs=(0, 512))
                for kt in range(8):
                    mm(cx, pa, wsb.ap(cs=(kt * DH + ot * 128, kt * DH + (ot + 1) * 128)),
                       xn.ap(cs=(kt * T + cs0, kt * T + cs0 + 512)), pbuf,
                       [wsb.buf, xn.buf], kt == 0, kt == 7)
                phi_from_psum(pa, pbuf, bcol.ap(cs=(ot, ot + 1)), bcol.buf,
                              dsttile.ap(cs=(ot * 512, (ot + 1) * 512)), dsttile.buf)
            for ot in range(4):
                dma(cx, "y", ddram[ot * 128:(ot + 1) * 128, cs0:cs0 + 512],
                    dsttile.ap(cs=(ot * 512, (ot + 1) * 512)), reads=[dsttile.buf], writes=[dbuf])
        for tt in range(4):
            ti = ch * 4 + tt
            pa, pbuf = cx.pbank(2, cs=(0, 512))
            for kt in range(8):
                mm(cx, pa, xn.ap(cs=(kt * T + ti * 128, kt * T + (ti + 1) * 128)),
                   wk_sb.ap(cs=(kt * DH, (kt + 1) * DH)), pbuf,
                   [wk_sb.buf, xnb[ti // 4]], kt == 0, kt == 7)
            if ch == 0 and tt == 0:
                bkrow = ar.alloc(F32, DH, parts=1, name="bk_row")
                bk_b = ar.alloc(F32, DH, name="bk_b")
                dma(cx, "y", bkrow.ap(ps=(0, 1)), io["bk_row"][:, :], writes=[bkrow.buf])
                bp, bpb = cx.pbank(3, cs=(0, 512))
                mm(cx, bp, cx.ones_row.ap(ps=(0, 1), cs=(0, 128)), bkrow.ap(ps=(0, 1)), bpb,
                   [bkrow.buf, cx.ones_row.buf], True, True)
                vop(cx, "tensor_copy", bk_b.ap(), bp, reads=[bpb], writes=[bk_b.buf])
                cx.bk_b = bk_b
            vop(cx, "tensor_tensor", ph0.ap(), pa, cx.bk_b.ap(), ALU.add,
                reads=[pbuf, cx.bk_b.buf], writes=[ph0.buf])
            vop(cx, "tensor_scalar", ph1.ap(), ph0.ap(), 0.0, None, ALU.min,
                reads=[ph0.buf], writes=[ph1.buf])
            act(cx, ph1.ap(), ph1.ap(), AF.Exp, [ph1.buf], [ph1.buf])
            vop(cx, "tensor_scalar", ph0.ap(), ph0.ap(), 1.0, None, ALU.add,
                reads=[ph0.buf], writes=[ph0.buf])
            vop(cx, "tensor_tensor", knt.ap(), ph0.ap(), ph1.ap(), ALU.max,
                reads=[ph0.buf, ph1.buf], writes=[knt.buf])
            dma(cx, "y", io["kN"][ti * 128:(ti + 1) * 128, :], knt.ap(), reads=[knt.buf],
                writes=[iob["kN"]])
            pv, pvb = cx.pwide(3)
            for kt in range(8):
                for half in range(2):
                    mm(cx, cx.pw[3][:, half * 512:(half + 1) * 512],
                       xn.ap(cs=(kt * T + ti * 128, kt * T + (ti + 1) * 128)),
                       wv_sb.ap(cs=(kt * D + half * 512, kt * D + (half + 1) * 512)),
                       pvb[half], [wv_sb.buf, xnb[ti // 4]], kt == 0, kt == 7)
            memset(cx, vnt.ap().rearrange("p (h u) -> p h u", u=G + 1)[:, :, G:G + 1], 1.0,
                   [vnt.buf])
            vop(cx, "tensor_tensor",
                vnt.ap().rearrange("p (h u) -> p h u", u=G + 1)[:, :, 0:G],
                cx.pw[3][:, :].rearrange("p (h u) -> p h u", u=G),
                bv_b.ap().rearrange("p (h u) -> p h u", u=G),
                ALU.add, reads=[pvb[0], pvb[1], bv_b.buf], writes=[vnt.buf])
            dma(cx, "y", io["vN"][ti * 128:(ti + 1) * 128, :], vnt.ap(), reads=[vnt.buf],
                writes=[iob["vN"]])
            pm, pmb = cx.pbank(4, cs=(0, 4 * (G + 2)))
            for h in range(H):
                base = (h % 4) * 32
                cblk = (h // 4) * (G + 2)
                mm(cx, cx.pb[4][0][base:base + 32, cx.pb[4][1] + cblk:cx.pb[4][1] + cblk + G + 1],
                   knt.ap(cs=(h * F, (h + 1) * F)),
                   vnt.ap(cs=(h * (G + 1), (h + 1) * (G + 1))), pmb,
                   [knt.buf, vnt.buf], True, True, tp=(0, base))
            mv = m_sb.ap().rearrange("p (b u) -> p b u", u=G + 2)[:, :, 0:G + 1]
            pv = pm.rearrange("p (b u) -> p b u", u=G + 2)[:, :, 0:G + 1]
            vop(cx, "tensor_tensor", mv, mv, pv, ALU.add,
                reads=[m_sb.buf, pmb], writes=[m_sb.buf])
    dma(cx, "y", io["mtot"][:, :], m_sb.ap(), reads=[m_sb.buf], writes=[iob["mtot"]])
    ar.pop()
    pr.barrier(skip=("g",))
    for nm_s, nm_d, n_ot, nk in (("h2t", "h2b", 32, 32), ("w3t", "w3b", 8, 32)):
        bl = []
        for ot in range(n_ot):
            b = Buf(f"{nm_d}{ot}")
            srcap = io[nm_s][:, ot * 128:(ot + 1) * 128].rearrange("(kt p) o -> p kt o", p=128)
            dma(cx, "g", io[nm_d][ot].rearrange("p (kt o) -> p kt o", kt=nk), srcap,
                reads=[iob[nm_s]], writes=[b])
            bl.append(b)
        cx.pc[nm_d] = bl

    TTF = 1024
    ar.push()
    h_a = ar.alloc(BF16, 32 * TTF, name="h_a")
    h_b = ar.alloc(BF16, 32 * TTF, name="h_b")
    sil = ar.alloc(BF16, TTF, name="sil")
    wbufA = [ar.alloc(BF16, 8 * 128, name=f"wA{i}") for i in range(2)]
    wbufB = [ar.alloc(BF16, 8 * 128, name=f"wB{i}") for i in range(2)]
    wbufH = [ar.alloc(BF16, 32 * 128, name=f"wH{i}") for i in range(2)]
    s_sb = {}
    for nm, dr in (("w1s", io["w1s"]), ("w2s", io["w2s"]), ("h1s", io["h1s"]), ("h2s", io["h2s"]), ("w3s", io["w3s"])):
        t = ar.alloc(F32, dr.shape[1], name=nm)
        dma(cx, "y", t.ap(), dr[:, :], writes=[t.buf])
        s_sb[nm] = t
    xre = [ar.alloc(F32, TTF, name=f"xre{i}") for i in range(2)]
    pxo = [ar.alloc(F32, TTF, name=f"pxo{i}") for i in range(2)]

    for tt in range(2):
        tok0 = tt * TTF
        for ot in range(32):
            wa = wbufA[ot % 2]
            wb = wbufB[ot % 2]
            dma(cx, "y", wa.ap(), io["w1b"][ot], reads=[cx.pc["w1b"][ot]], writes=[wa.buf])
            dma(cx, "y", wb.ap(), io["w2b"][ot], reads=[cx.pc["w2b"][ot]], writes=[wb.buf])
            pA, pAb = cx.pwide(0 if ot % 2 == 0 else 1)
            pB, pBb = cx.pwide(2 if ot % 2 == 0 else 3)
            for kt in range(8):
                for half in range(2):
                    cs = (half * 512, (half + 1) * 512)
                    xb_ = xnb[(tok0 + half * 512) // 512]
                    mm(cx, cx.pw[(0 if ot % 2 == 0 else 1)][:, cs[0]:cs[1]],
                       wa.ap(cs=(kt * 128, (kt + 1) * 128)),
                       xn.ap(cs=(kt * T + tok0 + half * 512, kt * T + tok0 + (half + 1) * 512)),
                       pAb[half], [wa.buf, xb_], kt == 0, kt == 7)
                    mm(cx, cx.pw[(2 if ot % 2 == 0 else 3)][:, cs[0]:cs[1]],
                       wb.ap(cs=(kt * 128, (kt + 1) * 128)),
                       xn.ap(cs=(kt * T + tok0 + half * 512, kt * T + tok0 + (half + 1) * 512)),
                       pBb[half], [wb.buf, xb_], kt == 0, kt == 7)
            act(cx, sil.ap(), pA, AF.Silu, [pAb[0], pAb[1], s_sb["w1s"].buf], [sil.buf],
                scale=s_sb["w1s"].ap(cs=(ot, ot + 1)))
            vop(cx, "scalar_tensor_tensor", h_a.ap(cs=(ot * TTF, (ot + 1) * TTF)),
                pB, s_sb["w2s"].ap(cs=(ot, ot + 1)), sil.ap(), ALU.mult, ALU.mult,
                reads=[pBb[0], pBb[1], s_sb["w2s"].buf, sil.buf], writes=[h_a.buf])
        for stage, (wnm, snm, src, dst) in enumerate((
                ("h1b", "h1s", h_a, h_b),
                ("h2b", "h2s", h_b, h_a))):
            for ot in range(32):
                wh = wbufH[ot % 2]
                dma(cx, "y", wh.ap(), io[wnm][ot], reads=[cx.pc[wnm][ot]], writes=[wh.buf])
                pA, pAb = cx.pwide(0 if ot % 2 == 0 else 1)
                for kt in range(32):
                    for half in range(2):
                        mm(cx, cx.pw[(0 if ot % 2 == 0 else 1)][:, half * 512:(half + 1) * 512],
                           wh.ap(cs=(kt * 128, (kt + 1) * 128)),
                           src.ap(cs=(kt * TTF + half * 512, kt * TTF + (half + 1) * 512)),
                           pAb[half], [wh.buf, src.buf], kt == 0, kt == 31)
                act(cx, dst.ap(cs=(ot * TTF, (ot + 1) * TTF)), pA, AF.Gelu,
                    [pAb[0], pAb[1], s_sb[snm].buf], [dst.buf],
                    scale=s_sb[snm].ap(cs=(ot, ot + 1)))
        for dt in range(8):
            xr = xre[dt % 2]
            po = pxo[dt % 2]
            dma(cx, "y", xr.ap(), io["xT"][dt * 128:(dt + 1) * 128, tok0:tok0 + TTF],
                reads=[iob["xT"]], writes=[xr.buf])
            pA, pAb = cx.pwide(0 if dt % 2 == 0 else 1)
            wh = wbufH[dt % 2]
            dma(cx, "y", wh.ap(), io["w3b"][dt], reads=[cx.pc["w3b"][dt]], writes=[wh.buf])
            for kt in range(32):
                for half in range(2):
                    mm(cx, cx.pw[(0 if dt % 2 == 0 else 1)][:, half * 512:(half + 1) * 512],
                       wh.ap(cs=(kt * 128, (kt + 1) * 128)),
                       h_a.ap(cs=(kt * TTF + half * 512, kt * TTF + (half + 1) * 512)),
                       pAb[half], [wh.buf, h_a.buf], kt == 0, kt == 31)
            vop(cx, "scalar_tensor_tensor", po.ap(), pA, s_sb["w3s"].ap(cs=(dt, dt + 1)),
                xr.ap(), ALU.mult, ALU.add,
                reads=[pAb[0], pAb[1], s_sb["w3s"].buf, xr.buf], writes=[po.buf])
            dma(cx, "y", io["px1"][dt * 128:(dt + 1) * 128, tok0:tok0 + TTF], po.ap(),
                reads=[po.buf], writes=[iob["px1"]])
    ar.pop()
    pr.barrier(skip=("g",))



L1_INS = [("xT", [D, T], "f"), ("wq", [D, DH], "f"), ("wk", [D, DH], "f"),
          ("wv", [D, D], "f"), ("bq", [128, 4], "f"), ("bk", [128, 4], "f"),
          ("bk_row", [1, DH], "f"), ("bv", [1, D], "f"),
          ("w1t", [D, DF], "f"), ("w2t", [D, DF], "f"), ("h1t", [DF, DF], "f"),
          ("h2t", [DF, DF], "f"), ("w3t", [DF, D], "f"),
          ("w1s", [128, DF // 128], "f"), ("w2s", [128, DF // 128], "f"),
          ("h1s", [128, DF // 128], "f"), ("h2s", [128, DF // 128], "f"),
          ("w3s", [128, D // 128], "f")]
L1_MID = [("px1", [D, T], "f"), ("qT", [DH, T], "b"), ("kT", [DH, T], "b"),
          ("kN", [T, DH], "b"), ("vN", [T, H * (G + 1)], "b"),
          ("mtot", [128, 4 * (G + 2)], "f")]
L1_PRECAST = [("w1b", [32, 128, 8 * 128], "b"), ("w2b", [32, 128, 8 * 128], "b"),
              ("h1b", [32, 128, 32 * 128], "b"), ("h2b", [32, 128, 32 * 128], "b"),
              ("w3b", [8, 128, 32 * 128], "b")]


def _declare(nc, specs, kind):
    out = {}
    for name, shape, dt in specs:
        dtt = F32 if dt == "f" else BF16
        if kind is None:
            out[name] = nc.dram_tensor(name, shape, dtt)
        else:
            out[name] = nc.dram_tensor(name, shape, dtt, kind=kind)
    return out


def build_launch1(debug=False):
    nc = bass.Bass("TRN2", target_bir_lowering=False, debug=False)
    import contextlib
    ctx = contextlib.ExitStack()
    cx = Ctx(nc, ctx, arena_bytes=211500)
    io = {}
    io.update(_declare(nc, L1_INS, "ExternalInput"))
    io.update(_declare(nc, L1_MID, "ExternalOutput"))
    io.update(_declare(nc, L1_PRECAST, None))
    io["xn_dbg"] = nc.dram_tensor("xn_dbg", [D, T], BF16, kind="ExternalOutput") if debug else None
    iob = {n: Buf(n) for n in list(io)}
    with ctx:
        _emit_l1(cx, io, iob, debug=debug)
        with nc.Block() as block, _sems(nc, ctx, cx.prog.sems) as sems:
            cx.prog.flush(block, sems, nc=nc)
    return nc


def _sems(nc, ctx, names):
    import contextlib

    @contextlib.contextmanager
    def gen():
        with contextlib.ExitStack() as es:
            yield {n: es.enter_context(nc.semaphore(f"sem_{n}")) for n in names}
    return gen()



def _emit_l2(cx, io, iob, init_state, debug=False, post_qkuv=None):
    nc, pr = cx.nc, cx.prog
    GS = G + 2
    alloc_consts(cx)
    ar = cx.ar
    attn0 = ar.alloc(BF16, 8 * T, name="attn0")
    attn0b = [Buf(f"attn0q{i}") for i in range(4)]
    x1qb = [Buf(f"x1q{i}") for i in range(4)]
    xn1b = [Buf(f"xn1q{i}") for i in range(4)]

    x1 = ar.alloc(F32, 8 * T, name="x1")
    ar.push()
    qT = ar.alloc(BF16, 4 * T, name="qT")
    kT = ar.alloc(BF16, 4 * T, name="kT")
    kN = ar.alloc(BF16, 16 * DH, name="kN")
    vN = ar.alloc(BF16, 16 * H * (G + 1), name="vN")
    tri = ar.alloc(F32, 128, name="tri")
    m_sb = ar.alloc(F32, 4 * GS, name="m_sb")
    m_bf = ar.alloc(BF16, 4 * GS, name="m_bf")
    mp = [ar.alloc(F32, 4 * GS, name=f"mp{i}") for i in range(2)]
    a_bf2 = [ar.alloc(BF16, 128, name=f"a_bf{i}") for i in range(2)]
    zrow2 = [ar.alloc(F32, 128, parts=1, name=f"zrow{i}") for i in range(2)]
    zscr2 = [ar.alloc(F32, 128, parts=1, name=f"zscr{i}") for i in range(2)]
    zb_sb2 = [ar.alloc(F32, 128, name=f"zb_sb{i}") for i in range(2)]
    dma(cx, "y", qT.ap().rearrange("p (dt t) -> p dt t", dt=4),
        io["qT"].ap().rearrange("(dt p) t -> p dt t", p=128),
        reads=[iob["qT"]], writes=[qT.buf])
    dma(cx, "y", kT.ap().rearrange("p (dt t) -> p dt t", dt=4),
        io["kT"].ap().rearrange("(dt p) t -> p dt t", p=128),
        reads=[iob["kT"]], writes=[kT.buf])
    dma(cx, "y", kN.ap().rearrange("p (n c) -> p n c", n=16),
        io["kN"].ap().rearrange("(n p) c -> p n c", p=128),
        reads=[iob["kN"]], writes=[kN.buf])
    dma(cx, "y", vN.ap().rearrange("p (n c) -> p n c", n=16),
        io["vN"].ap().rearrange("(n p) c -> p n c", p=128),
        reads=[iob["vN"]], writes=[vN.buf])
    dma(cx, "y", tri.ap(), io["tri"][:, :], reads=[iob["tri"]], writes=[tri.buf])
    init_state(cx, m_sb, mp)
    vop(cx, "tensor_copy", m_bf.ap(), m_sb.ap(), reads=[m_sb.buf], writes=[m_bf.buf])

    for c in range(NCH):
        c0 = c * C
        for h in range(H):
            base = (h % 4) * 32
            cb = (h // 4) * GS
            qtile = h // 4
            qsl = qT.ap(ps=(base, base + 32), cs=(qtile * T + c0, qtile * T + c0 + C))
            ksl = kT.ap(ps=(base, base + 32), cs=(qtile * T + c0, qtile * T + c0 + C))
            a_bf = a_bf2[h % 2]
            zrow = zrow2[h % 2]
            zb_sb = zb_sb2[h % 2]
            pa, pab = cx.pbank(0 if h % 2 == 0 else 1, cs=(0, 128))
            mm(cx, pa, ksl, qsl, pab, [kT.buf, qT.buf], True, True, tp=(base, 0))
            vop(cx, "tensor_tensor", a_bf.ap(), pa, tri.ap(), ALU.mult,
                reads=[pab, tri.buf], writes=[a_bf.buf])
            pt, ptb = cx.pbank(2 if h % 2 == 0 else 3, cs=(0, 128))
            pt65 = cx.pb[2 if h % 2 == 0 else 3]
            att_ap = pt65[0][0:65, pt65[1]:pt65[1] + 128]
            mm(cx, att_ap, vN.ap(cs=(c * H * (G + 1) + h * (G + 1),
                                     c * H * (G + 1) + (h + 1) * (G + 1))),
               a_bf.ap(), ptb, [vN.buf, a_bf.buf], True, False)
            mm(cx, att_ap, m_bf.ap(ps=(base, base + 32), cs=(cb, cb + G + 1)), qsl,
               ptb, [m_bf.buf, qT.buf], False, True, tp=(base, 0))
            zsrc = pt65[0][64:65, pt65[1]:pt65[1] + 128]
            vop(cx, "tensor_scalar", zrow.ap(ps=(0, 1)), zsrc, 1e-6, None, ALU.add,
                reads=[ptb], writes=[zrow.buf])
            zscr = zscr2[h % 2]
            recip_fast(cx, zrow.ap(ps=(0, 1)), zrow.ap(ps=(0, 1)), zscr.ap(ps=(0, 1)),
                       [zrow.buf], [zrow.buf, zscr.buf])
            zbank = 4 if h % 2 == 0 else 6
            pz, pzb = cx.pbank(zbank, cs=(0, 128))
            pz64 = cx.pb[zbank][0][0:64, cx.pb[zbank][1]:cx.pb[zbank][1] + 128]
            mm(cx, pz64, cx.ones_row.ap(ps=(0, 1), cs=(0, 64)), zrow.ap(ps=(0, 1)),
               pzb, [cx.ones_row.buf, zrow.buf], True, True)
            prog_eng = "s"
            cx.prog.emit(prog_eng, lambda hh, o=zb_sb.ap(ps=(0, 64)), i=pz64: hh.copy(o, i),
                         reads=[pzb], writes=[zb_sb.buf])
            gtile, gbase = h // 2, (h % 2) * 64
            strip = attn0.ap(ps=(gbase, gbase + 64), cs=(gtile * T + c0, gtile * T + c0 + C))
            vop(cx, "tensor_tensor", strip,
                pt65[0][0:64, pt65[1]:pt65[1] + 128], zb_sb.ap(ps=(0, 64)), ALU.mult,
                reads=[ptb, zb_sb.buf], writes=[attn0.buf])
            pd, pdb = cx.pbank(5, cs=(0, 4 * GS))
            mm(cx, cx.pb[5][0][base:base + 32, cx.pb[5][1] + cb:cx.pb[5][1] + cb + G + 1],
               kN.ap(cs=(c * DH + h * F, c * DH + (h + 1) * F)),
               vN.ap(cs=(c * H * (G + 1) + h * (G + 1),
                         c * H * (G + 1) + (h + 1) * (G + 1))),
               pdb, [kN.buf, vN.buf], True, True, tp=(0, base))
        mv = m_sb.ap().rearrange("p (b u) -> p b u", u=GS)[:, :, 0:G + 1]
        pv = cx.pbank(5, cs=(0, 4 * GS))[0].rearrange("p (b u) -> p b u", u=GS)[:, :, 0:G + 1]
        vop(cx, "tensor_tensor", mv, mv, pv, ALU.add,
            reads=[m_sb.buf, cx.pb[5][2]], writes=[m_sb.buf])
        vop(cx, "tensor_copy", m_bf.ap(), m_sb.ap(), reads=[m_sb.buf], writes=[m_bf.buf])
    if debug:
        for dt in range(8):
            dma(cx, "y", io["a0_dbg"][dt * 128:(dt + 1) * 128, :],
                attn0.ap(cs=(dt * T, (dt + 1) * T)), reads=attn0b, writes=[iob["a0_dbg"]])

    ar.push()
    wo_sb = ar.alloc(BF16, 8 * D, name="wo_sb")
    bo_sb = ar.alloc(F32, 8, name="bo_sb")
    pxb = [ar.alloc(F32, 512, name=f"pxb{i}") for i in range(2)]
    dma(cx, "g", wo_sb.ap().rearrange("p (gt o) -> p gt o", gt=8),
        io["wo"].ap().rearrange("(gt p) o -> p gt o", p=128), reads=[iob["wo"]], writes=[wo_sb.buf])
    dma(cx, "y", bo_sb.ap(), io["bo"][:, :], reads=[iob["bo"]], writes=[bo_sb.buf])
    for qc in range(4):
        cs0 = qc * 512
        for dt in range(8):
            xb = pxb[dt % 2]
            dma(cx, "y", xb.ap(), io["px1"][dt * 128:(dt + 1) * 128, cs0:cs0 + 512],
                reads=[iob["px1"]], writes=[xb.buf])
            pa, pab = cx.pbank(0 if dt % 2 == 0 else 1)
            for gt in range(8):
                mm(cx, pa, wo_sb.ap(cs=(gt * D + dt * 128, gt * D + (dt + 1) * 128)),
                   attn0.ap(cs=(gt * T + cs0, gt * T + cs0 + 512)), pab,
                   [wo_sb.buf, attn0b[qc]], gt == 0, gt == 7)
            vop(cx, "scalar_tensor_tensor", x1.ap(cs=(dt * T + cs0, dt * T + cs0 + 512)),
                pa, bo_sb.ap(cs=(dt, dt + 1)), xb.ap(), ALU.add, ALU.add,
                reads=[pab, bo_sb.buf, xb.buf], writes=[x1qb[qc]])
    ar.pop()
    ar.pop()
    pr.barrier(skip=("g",))
    if debug:
        for dt in range(8):
            dma(cx, "y", io["x1_dbg"][dt * 128:(dt + 1) * 128, :],
                x1.ap(cs=(dt * T, (dt + 1) * T)), reads=x1qb, writes=[iob["x1_dbg"]])
    xn1 = ar.alloc(BF16, 8 * T, name="xn1")

    ar.push()
    cx.ln_xsq2 = [ar.alloc(F32, 512, name=f"ln_xsq{i}") for i in range(2)]
    cx.ln_rows2 = [ar.alloc(F32, 4 * 512, parts=1, name=f"ln_rows{i}") for i in range(2)]
    for ch in range(4):
        tiles = [x1.ap(cs=(dt * T + ch * 512, dt * T + ch * 512 + 512)) for dt in range(8)]
        emit_layernorm(cx, tiles, [x1qb[ch]] * 8, xn1, dst_off=ch * 512, wbuf=xn1b[ch],
                       pbase=(0 if ch % 2 == 0 else 3), scr=ch % 2)
    pr.barrier()
    ar.pop()

    ar.push()
    wbuf = [ar.alloc(BF16, 8 * 128, name=f"wq{i}") for i in range(2)]
    strip = [ar.alloc(BF16, 512, name=f"qs{i}") for i in range(2)]
    for gi, (col0, dst, dbuf) in enumerate(
            ((0, io["qTf"], iob["qTf"]), (D, io["kTf"], iob["kTf"]), (3 * D, io["gTf"], iob["gTf"]))):
        for ot in range(8):
            wb = wbuf[ot % 2]
            src = io["wquv"][:, col0 + ot * 128: col0 + (ot + 1) * 128].rearrange(
                "(kt p) o -> p kt o", p=128)
            dma(cx, "g", wb.ap().rearrange("p (kt o) -> p kt o", kt=8), src,
                reads=[iob["wquv"]], writes=[wb.buf])
            for qc in range(4):
                st = strip[qc % 2]
                pa, pab = cx.pbank(qc % 2)
                for kt in range(8):
                    mm(cx, pa, wb.ap(cs=(kt * 128, (kt + 1) * 128)),
                       xn1.ap(cs=(kt * T + qc * 512, kt * T + qc * 512 + 512)),
                       pab, [wb.buf, xn1.buf], kt == 0, kt == 7)
                act(cx, st.ap(), pa, AF.Silu, [pab], [st.buf])
                dma(cx, "y", dst[ot * 128:(ot + 1) * 128, qc * 512:(qc + 1) * 512],
                    st.ap(), reads=[st.buf], writes=[dbuf])
    wv_sb = ar.alloc(BF16, 8 * D, name="wv_sb")
    vstrip = [ar.alloc(BF16, D, name=f"vs{i}") for i in range(2)]
    dma(cx, "g", wv_sb.ap().rearrange("p (kt o) -> p kt o", kt=8),
        io["wquv"][:, 2 * D:3 * D].rearrange("(kt p) o -> p kt o", p=128),
        reads=[iob["wquv"]], writes=[wv_sb.buf])
    for tt in range(16):
        vs = vstrip[tt % 2]
        pa, pab = cx.pwide(1 if tt % 2 == 0 else 0)
        pwi = 1 if tt % 2 == 0 else 0
        for kt in range(8):
            for half in range(2):
                mm(cx, cx.pw[pwi][:, half * 512:(half + 1) * 512],
                   xn1.ap(cs=(kt * T + tt * 128, kt * T + (tt + 1) * 128)),
                   wv_sb.ap(cs=(kt * D + half * 512, kt * D + (half + 1) * 512)),
                   pab[half], [wv_sb.buf, xn1b[tt // 4]], kt == 0, kt == 7)
        act(cx, vs.ap(), pa, AF.Silu, [pab[0], pab[1]], [vs.buf])
        dma(cx, "y", io["vNf"][tt * 128:(tt + 1) * 128, :], vs.ap(),
            reads=[vs.buf], writes=[iob["vNf"]])
    if post_qkuv is not None:
        post_qkuv()
    ar.pop()
    pr.barrier(skip=("g", "c"))

    ar.push()
    projT = ar.alloc(F32, 2 * T, name="projT")
    projn = ar.alloc(BF16, 2 * T, name="projn")
    psq = ar.alloc(F32, 512, name="psq")
    rows2 = ar.alloc(F32, 2 * T, parts=1, name="rows2")
    wpt_sb = ar.alloc(BF16, 8 * 256, name="wpt_sb")
    bp_sb = ar.alloc(F32, 2, name="bp_sb")
    sim_sb = ar.alloc(F32, 8, name="sim_sb")
    simsc = ar.alloc(BF16, 8, name="simsc")
    crow = ar.alloc(F32, 8, parts=1, name="crow")
    emat = ar.alloc(F32, 128, parts=4, name="emat")
    retT4 = ar.alloc(F32, T, parts=4, name="retT4")
    l_sb = ar.alloc(F32, 4, name="l_sb")
    e_sb = ar.alloc(F32, 4, name="e_sb")
    red1 = ar.alloc(F32, 4, name="red1")
    ret_f = ar.alloc(F32, 4, name="ret_f")
    ret_bf = ar.alloc(BF16, 4, name="ret_bf")
    eye = ar.alloc(F32, 128, name="eye")
    retb_sb = ar.alloc(BF16, 512, name="retb_sb")
    xds = ar.alloc(BF16, 4 * 512, name="xds")
    wd_sb = ar.alloc(BF16, 8 * 128, name="wd_sb")
    wu_sb = ar.alloc(BF16, D, name="wu_sb")
    wt_b = [ar.alloc(BF16, 8 * 128, name=f"wt{i}") for i in range(2)]
    x1bo = [ar.alloc(F32, 512, name=f"x1bo{i}") for i in range(2)]

    dma(cx, "g", wpt_sb.ap().rearrange("p (kt o) -> p kt o", kt=8),
        io["Wpt"].ap().rearrange("(kt p) o -> p kt o", p=128), reads=[iob["Wpt"]], writes=[wpt_sb.buf])
    dma(cx, "y", bp_sb.ap(), io["bp"][:, :], reads=[iob["bp"]], writes=[bp_sb.buf])
    dma(cx, "y", sim_sb.ap().rearrange("p (pt e) -> p pt e", pt=2),
        io["sim"].ap().rearrange("(pt p) e -> p pt e", p=128), reads=[iob["sim"]], writes=[sim_sb.buf])
    dma(cx, "y", crow.ap(ps=(0, 1), cs=(4, 5)), io["temp"][:, :], reads=[iob["temp"]], writes=[crow.buf])
    dma(cx, "y", emat.ap(ps=(0, 4)), io["emat"][:, :], reads=[iob["emat"]], writes=[emat.buf])
    dma(cx, "g", wd_sb.ap().rearrange("p (kt o) -> p kt o", kt=8),
        io["wdt"].ap().rearrange("(kt p) o -> p kt o", p=128), reads=[iob["wdt"]], writes=[wd_sb.buf])
    dma(cx, "g", wu_sb.ap(), io["wut"][:, :], reads=[iob["wut"]], writes=[wu_sb.buf])

    for pt_i in range(2):
        for qc in range(4):
            pa, pab = cx.pbank(qc % 2)
            for kt in range(8):
                mm(cx, pa, wpt_sb.ap(cs=(kt * 256 + pt_i * 128, kt * 256 + (pt_i + 1) * 128)),
                   xn1.ap(cs=(kt * T + qc * 512, kt * T + qc * 512 + 512)),
                   pab, [wpt_sb.buf, xn1b[qc]], kt == 0, kt == 7)
            vop(cx, "tensor_scalar", projT.ap(cs=(pt_i * T + qc * 512, pt_i * T + qc * 512 + 512)),
                pa, bp_sb.ap(cs=(pt_i, pt_i + 1)), None, ALU.add,
                reads=[pab, bp_sb.buf], writes=[projT.buf])
    for qc in range(4):
        ps_st, ps_stb = cx.pbank(2, cs=(0, 512))
        for pt_i in range(2):
            act(cx, psq.ap(), projT.ap(cs=(pt_i * T + qc * 512, pt_i * T + qc * 512 + 512)),
                AF.Square, [projT.buf], [psq.buf])
            mm(cx, ps_st[0:1, :], cx.ones_col.ap(), psq.ap(), ps_stb,
               [cx.ones_col.buf, psq.buf], pt_i == 0, pt_i == 1)
        nrm = rows2.ap(ps=(0, 1), cs=(T + qc * 512, T + qc * 512 + 512))
        act(cx, nrm, ps_st[0:1, :], AF.Sqrt, [ps_stb], [rows2.buf])
        pinv = rows2.ap(ps=(0, 1), cs=(qc * 512, qc * 512 + 512))
        vop(cx, "tensor_scalar", nrm, nrm, 1e-12, None, ALU.max,
            reads=[rows2.buf], writes=[rows2.buf])
        recip_fast(cx, pinv, nrm, psq.ap(ps=(0, 1)), [rows2.buf], [rows2.buf, psq.buf])
        pz, pzb = cx.pbank(3, cs=(0, 512))
        mm(cx, pz, cx.ones_row.ap(ps=(0, 1), cs=(0, 128)), pinv, pzb,
           [cx.ones_row.buf, rows2.buf], True, True)
        for pt_i in range(2):
            vop(cx, "tensor_tensor", projn.ap(cs=(pt_i * T + qc * 512, pt_i * T + qc * 512 + 512)),
                projT.ap(cs=(pt_i * T + qc * 512, pt_i * T + qc * 512 + 512)), pz, ALU.mult,
                reads=[projT.buf, pzb], writes=[projn.buf])
    ps_st, ps_stb = cx.pbank(2, cs=(0, 8))
    for pt_i in range(2):
        act(cx, psq.ap(cs=(0, 4)), sim_sb.ap(cs=(pt_i * 4, (pt_i + 1) * 4)), AF.Square,
            [sim_sb.buf], [psq.buf])
        mm(cx, ps_st[0:1, 0:4], cx.ones_col.ap(), psq.ap(cs=(0, 4)), ps_stb,
           [cx.ones_col.buf, psq.buf], pt_i == 0, pt_i == 1)
    cn = crow.ap(ps=(0, 1), cs=(0, 4))
    act(cx, cn, ps_st[0:1, 0:4], AF.Sqrt, [ps_stb], [crow.buf])
    vop(cx, "tensor_scalar", cn, cn, 1e-12, None, ALU.max, reads=[crow.buf], writes=[crow.buf])
    vop(cx, "reciprocal", cn, cn, reads=[crow.buf], writes=[crow.buf])
    tmp1 = crow.ap(ps=(0, 1), cs=(4, 5))
    vop(cx, "tensor_scalar", tmp1, tmp1, float(np.log(100.0)), None, ALU.min,
        reads=[crow.buf], writes=[crow.buf])
    act(cx, tmp1, tmp1, AF.Exp, [crow.buf], [crow.buf])
    vop(cx, "tensor_scalar", cn, cn, tmp1, None, ALU.mult, reads=[crow.buf], writes=[crow.buf])
    pz, pzb = cx.pbank(3, cs=(0, 4))
    mm(cx, pz, cx.ones_row.ap(ps=(0, 1), cs=(0, 128)), cn, pzb,
       [cx.ones_row.buf, crow.buf], True, True)
    for pt_i in range(2):
        vop(cx, "tensor_tensor", simsc.ap(cs=(pt_i * 4, (pt_i + 1) * 4)),
            sim_sb.ap(cs=(pt_i * 4, (pt_i + 1) * 4)), pz, ALU.mult,
            reads=[sim_sb.buf, pzb], writes=[simsc.buf])
    dma(cx, "y", eye.ap(), io["eye"][:, :], reads=[iob["eye"]], writes=[eye.buf])
    for tt in range(16):
        pa, pab = cx.pbank(tt % 2, cs=(0, 4))
        for pt_i in range(2):
            mm(cx, pa, projn.ap(cs=(pt_i * T + tt * 128, pt_i * T + (tt + 1) * 128)),
               simsc.ap(cs=(pt_i * 4, (pt_i + 1) * 4)), pab,
               [projn.buf, simsc.buf], pt_i == 0, pt_i == 1)
        vop(cx, "tensor_copy", l_sb.ap(), pa, reads=[pab], writes=[l_sb.buf])
        vop(cx, "tensor_reduce", red1.ap(cs=(0, 1)), l_sb.ap(), AX.X, ALU.max,
            reads=[l_sb.buf], writes=[red1.buf], negate=True)
        act(cx, e_sb.ap(), l_sb.ap(), AF.Exp, [l_sb.buf, red1.buf], [e_sb.buf],
            bias=red1.ap(cs=(0, 1)))
        vop(cx, "tensor_reduce", red1.ap(cs=(1, 2)), e_sb.ap(), AX.X, ALU.add,
            reads=[e_sb.buf], writes=[red1.buf])
        vop(cx, "reciprocal", red1.ap(cs=(2, 3)), red1.ap(cs=(1, 2)),
            reads=[red1.buf], writes=[red1.buf])
        vop(cx, "tensor_scalar", ret_f.ap(), e_sb.ap(), red1.ap(cs=(2, 3)), None, ALU.mult,
            reads=[e_sb.buf, red1.buf], writes=[ret_f.buf])
        vop(cx, "tensor_reduce", red1.ap(cs=(3, 4)), ret_f.ap(), AX.X, ALU.max,
            reads=[ret_f.buf], writes=[red1.buf])
        vop(cx, "tensor_scalar", e_sb.ap(), ret_f.ap(), red1.ap(cs=(3, 4)), None, ALU.is_ge,
            reads=[ret_f.buf, red1.buf], writes=[e_sb.buf])
        vop(cx, "tensor_tensor", ret_f.ap(), ret_f.ap(), e_sb.ap(), ALU.mult,
            reads=[ret_f.buf, e_sb.buf], writes=[ret_f.buf])
        pb_t, pb_tb = cx.pbank(4, cs=(0, 128))
        def tr(hh, o=cx.pb[4][0][0:4, cx.pb[4][1]:cx.pb[4][1] + 128], i=ret_f.ap(), e=eye.ap()):
            return hh.transpose(o, i, e)
        cx.prog.emit("t", tr, reads=[ret_f.buf, eye.buf], writes=[pb_tb])
        vop(cx, "tensor_copy", retT4.ap(ps=(0, 4), cs=(tt * 128, (tt + 1) * 128)),
            cx.pb[4][0][0:4, cx.pb[4][1]:cx.pb[4][1] + 128], reads=[pb_tb], writes=[retT4.buf])
    for qc in range(4):
        prb, prbb = cx.pbank(4, cs=(0, 512))
        mm(cx, prb, emat.ap(ps=(0, 4)), retT4.ap(ps=(0, 4), cs=(qc * 512, qc * 512 + 512)),
           prbb, [emat.buf, retT4.buf], True, True)
        vop(cx, "tensor_copy", retb_sb.ap(), prb, reads=[prbb], writes=[retb_sb.buf])
        pd, pdb = cx.pbank(5, cs=(0, 512))
        for kt in range(8):
            mm(cx, pd, wd_sb.ap(cs=(kt * 128, (kt + 1) * 128)),
               xn1.ap(cs=(kt * T + qc * 512, kt * T + qc * 512 + 512)),
               pdb, [wd_sb.buf, xn1.buf], kt == 0, kt == 7)
        vop(cx, "tensor_tensor", xds.ap(cs=(qc * 512, qc * 512 + 512)), pd, retb_sb.ap(),
            ALU.mult, reads=[pdb, retb_sb.buf], writes=[xds.buf])
    for dt in range(8):
        wb = wt_b[dt % 2]
        dma(cx, "g", wb.ap().rearrange("p (kt o) -> p kt o", kt=8),
            io["Wt"][:, dt * 128:(dt + 1) * 128].rearrange("(kt p) o -> p kt o", p=128),
            reads=[iob["Wt"]], writes=[wb.buf])
        for qc in range(4):
            xo = x1bo[qc % 2]
            pa, pab = cx.pbank(qc % 2)
            for kt in range(8):
                mm(cx, pa, wb.ap(cs=(kt * 128, (kt + 1) * 128)),
                   xn1.ap(cs=(kt * T + qc * 512, kt * T + qc * 512 + 512)),
                   pab, [wb.buf, xn1.buf], kt == 0, False)
            mm(cx, pa, wu_sb.ap(cs=(dt * 128, (dt + 1) * 128)),
               xds.ap(cs=(qc * 512, qc * 512 + 512)), pab,
               [wu_sb.buf, xds.buf], False, True)
            vop(cx, "tensor_tensor", xo.ap(), pa,
                x1.ap(cs=(dt * T + qc * 512, dt * T + qc * 512 + 512)), ALU.add,
                reads=[pab, x1.buf], writes=[xo.buf])
            dma(cx, "y", io["x1b"][dt * 128:(dt + 1) * 128, qc * 512:(qc + 1) * 512],
                xo.ap(), reads=[xo.buf], writes=[iob["x1b"]])
    ar.pop()
    pr.barrier()



L2_INS = [("tri", [128, 128], "f"), ("wo", [D, D], "f"), ("bo", [128, 8], "f"),
          ("wquv", [D, 4 * D], "f"), ("Wt", [D, D], "f"), ("Wpt", [D, 256], "f"),
          ("bp", [128, 2], "f"), ("sim", [256, 4], "f"), ("temp", [1, 1], "f"),
          ("wdt", [D, 128], "f"), ("wut", [128, D], "f"), ("emat", [4, 128], "f"),
          ("eye", [128, 128], "f")]
L2_MID = [("x1b", [D, T], "f"), ("qTf", [D, T], "b"), ("gTf", [D, T], "b"),
          ("kTf", [D, T], "b"), ("vNf", [T, D], "b")]


def build_launch2(debug=False):
    nc = bass.Bass("TRN2", target_bir_lowering=False, debug=False)
    import contextlib
    ctx = contextlib.ExitStack()
    cx = Ctx(nc, ctx, arena_bytes=211500)
    io = {}
    io.update(_declare(nc, L1_MID, "ExternalInput"))
    io.update(_declare(nc, L2_INS, "ExternalInput"))
    io["mpre"] = nc.dram_tensor("mpre", [3, 128, 4 * (G + 2)], F32, kind="ExternalInput")
    io.update(_declare(nc, L2_MID, "ExternalOutput"))
    io["a0_dbg"] = nc.dram_tensor("a0_dbg", [D, T], BF16, kind="ExternalOutput") if debug else None
    io["x1_dbg"] = nc.dram_tensor("x1_dbg", [D, T], F32, kind="ExternalOutput") if debug else None
    iob = {n: Buf(n) for n in list(io)}

    def init_state(cx2, m_sb, mp):
        dma(cx2, "y", m_sb.ap(), io["mpre"][0, :, :], reads=[iob["mpre"]], writes=[m_sb.buf])
        dma(cx2, "y", mp[0].ap(), io["mpre"][1, :, :], reads=[iob["mpre"]], writes=[mp[0].buf])
        dma(cx2, "y", mp[1].ap(), io["mpre"][2, :, :], reads=[iob["mpre"]], writes=[mp[1].buf])
        vop(cx2, "tensor_tensor", m_sb.ap(), m_sb.ap(), mp[0].ap(), ALU.add,
            reads=[m_sb.buf, mp[0].buf], writes=[m_sb.buf])
        vop(cx2, "tensor_tensor", m_sb.ap(), m_sb.ap(), mp[1].ap(), ALU.add,
            reads=[m_sb.buf, mp[1].buf], writes=[m_sb.buf])

    with ctx:
        _emit_l2(cx, io, iob, init_state, debug=debug)
        with nc.Block() as block, _sems(nc, ctx, cx.prog.sems) as sems:
            cx.prog.flush(block, sems, nc=nc)
    return nc



def _emit_l3(cx, io, iob, debug=False):
    nc, pr = cx.nc, cx.prog
    NK = SQ // 128
    alloc_consts(cx)
    ar = cx.ar
    gated = ar.alloc(BF16, 8 * T, name="gated")
    ar.push()
    qT = ar.alloc(BF16, 8 * T, name="qT")
    gT = ar.alloc(BF16, 8 * T, name="gT")
    gm2 = ar.alloc(F32, 1, name="gm2")
    bt2 = ar.alloc(F32, 1, name="bt2")
    att_pair = ar.alloc(F32, 2048, name="att_pair")
    att_sqp = ar.alloc(BF16, 2048, name="att_sqp")
    ss_sb = [ar.alloc(F32, 2048, parts=1, name=f"ss{i}") for i in range(2)]
    rstd_bf = [ar.alloc(F32, 2048, parts=1, name=f"rstd{i}") for i in range(2)]
    attn_t = ar.alloc(F32, 512, name="attn_t")
    w_sb = [ar.alloc(BF16, 1024, name=f"w{i}") for i in range(2)]
    kTp = [ar.alloc(BF16, SQ, name=f"kTp{i}") for i in range(2)]
    vp = [[ar.alloc(BF16, NK * G, name=f"vp{i}{j}") for j in range(2)] for i in range(1)]
    vbar2 = [ar.alloc(F32, NK, name=f"vbar{i}") for i in range(2)]
    dma(cx, "y", qT.ap().rearrange("p (dt t) -> p dt t", dt=8),
        io["qTf"].ap().rearrange("(dt p) t -> p dt t", p=128),
        reads=[iob["qTf"]], writes=[qT.buf])
    dma(cx, "y", gT.ap().rearrange("p (dt t) -> p dt t", dt=8),
        io["gTf"].ap().rearrange("(dt p) t -> p dt t", p=128),
        reads=[iob["gTf"]], writes=[gT.buf])
    dma(cx, "y", gm2.ap(), io["gamma2"][:, :], reads=[iob["gamma2"]], writes=[gm2.buf])
    dma(cx, "y", bt2.ap(), io["beta2"][:, :], reads=[iob["beta2"]], writes=[bt2.buf])

    for pair in range(H // 2):
        kb = kTp[pair % 2]
        vbs = vp[0]
        for hh in range(2):
            h = pair * 2 + hh
            for r in range(CPB):
                dma(cx, "y", kb.ap(ps=(hh * 64, hh * 64 + 64), cs=(r * T, (r + 1) * T)),
                    io["kTa"][r * D + h * 64: r * D + (h + 1) * 64, :],
                    reads=[iob["kTa"]], writes=[kb.buf])
            dma(cx, "y", vbs[hh].ap().rearrange("p (n d) -> p n d", n=NK),
                io["vNa"][:, h * 64:(h + 1) * 64].rearrange("(n p) d -> p n d", p=128),
                reads=[iob["vNa"]], writes=[vbs[hh].buf])
        for hh in range(2):
            vb = vbs[hh]
            vbr = vbar2[hh]
            vop(cx, "tensor_reduce", vbr.ap(), vb.ap().rearrange("p (n d) -> p n d", n=NK),
                AX.X, ALU.add, reads=[vb.buf], writes=[vbr.buf])
            vop(cx, "tensor_scalar", vbr.ap(), vbr.ap(), 1.0 / G, None, ALU.mult,
                reads=[vbr.buf], writes=[vbr.buf])
            vb3 = vb.ap().rearrange("p (n d) -> p n d", n=NK)
            vba = vbr.ap()
            vbb = bass.AP(vba.tensor, vba.offset, [vba.ap[0], [1, NK], [0, G]])
            vop(cx, "tensor_tensor", vb3, vb3, vbb, ALU.subtract,
                reads=[vb.buf, vbr.buf], writes=[vb.buf])
        att0b, att1b = cx.pb[4], cx.pb[5]
        att0_ap = att0b[0][0:64, att0b[1]:att0b[1] + 512]
        att1_ap = att1b[0][64:128, att1b[1]:att1b[1] + 512]
        for qc in range(4):
            cs0 = qc * 512
            qsl0 = qT.ap(ps=(0, 64), cs=(pair * T + cs0, pair * T + cs0 + 512))
            qsl1 = qT.ap(ps=(64, 128), cs=(pair * T + cs0, pair * T + cs0 + 512))

            def emit_mm1(kt):
                pwi = kt % 2
                psc, pscb = cx.pwide(pwi)
                mm(cx, cx.pw[pwi][:, 0:512],
                   kb.ap(ps=(0, 64), cs=(kt * 128, (kt + 1) * 128)), qsl0,
                   pscb[0], [kb.buf, qT.buf], True, True, tp=(0, 0))
                mm(cx, cx.pw[pwi][:, 512:1024],
                   kb.ap(ps=(64, 128), cs=(kt * 128, (kt + 1) * 128)), qsl1,
                   pscb[1], [kb.buf, qT.buf], True, True, tp=(64, 0))

            emit_mm1(0)
            for kt in range(NK):
                psc, pscb = cx.pwide(kt % 2)
                ws = w_sb[kt % 2]
                act(cx, ws.ap(), psc, AF.Silu, [pscb[0], pscb[1]], [ws.buf], scale=0.125)
                if kt + 1 < NK:
                    emit_mm1(kt + 1)
                mm(cx, att0_ap, vbs[0].ap(cs=(kt * G, (kt + 1) * G)), ws.ap(cs=(0, 512)),
                   att0b[2], [vbs[0].buf, ws.buf], kt == 0, kt == NK - 1, tp=(0, 0))
                mm(cx, att1_ap, vbs[1].ap(cs=(kt * G, (kt + 1) * G)), ws.ap(cs=(512, 1024)),
                   att1b[2], [vbs[1].buf, ws.buf], kt == 0, kt == NK - 1, tp=(0, 64))
            vop(cx, "tensor_copy", att_pair.ap(ps=(0, 64), cs=(cs0, cs0 + 512)), att0_ap,
                reads=[att0b[2]], writes=[att_pair.buf])
            vop(cx, "tensor_copy", att_pair.ap(ps=(64, 128), cs=(cs0, cs0 + 512)), att1_ap,
                reads=[att1b[2]], writes=[att_pair.buf])
        vop(cx, "tensor_tensor", att_sqp.ap(), att_pair.ap(), att_pair.ap(), ALU.mult,
            reads=[att_pair.buf], writes=[att_sqp.buf])
        for hh in range(2):
            for qc in range(4):
                pss, pssb = cx.pbank(6, cs=(0, 512))
                mm(cx, pss[0:1, :], cx.ones_bf.ap(ps=(hh * 64, hh * 64 + 64)),
                   att_sqp.ap(ps=(hh * 64, hh * 64 + 64), cs=(qc * 512, qc * 512 + 512)),
                   pssb, [cx.ones_bf.buf, att_sqp.buf], True, True, tp=(hh * 64, 0))
                cx.prog.emit("s", lambda h2, o=ss_sb[hh].ap(ps=(0, 1), cs=(qc * 512, qc * 512 + 512)),
                             i=pss[0:1, :]: h2.copy(o, i),
                             reads=[pssb], writes=[ss_sb[hh].buf])
        for hh in range(2):
            act(cx, ss_sb[hh].ap(ps=(0, 1)), ss_sb[hh].ap(ps=(0, 1)), AF.Sqrt,
                [ss_sb[hh].buf], [ss_sb[hh].buf],
                scale=1.0 / G, bias=cx.eps_row.ap(ps=(0, 1), cs=(0, 1)))
            recip_fast(cx, rstd_bf[hh].ap(ps=(0, 1)), ss_sb[hh].ap(ps=(0, 1)),
                       rstd_bf[1 - hh].ap(ps=(0, 1)),
                       [ss_sb[hh].buf], [rstd_bf[hh].buf, rstd_bf[1 - hh].buf])
        for hh in range(2):
            base = hh * 64
            for qc in range(4):
                cs0 = qc * 512
                prs, prsb = cx.pbank(7, cs=(0, 512))
                prs64 = cx.pb[7][0][0:64, cx.pb[7][1]:cx.pb[7][1] + 512]
                mm(cx, prs64, cx.ones_row.ap(ps=(0, 1), cs=(0, 64)),
                   rstd_bf[hh].ap(ps=(0, 1), cs=(cs0, cs0 + 512)), prsb,
                   [cx.ones_row.buf, rstd_bf[hh].buf], True, True)
                vop(cx, "tensor_tensor", attn_t.ap(ps=(base, base + 64)),
                    att_pair.ap(ps=(base, base + 64), cs=(cs0, cs0 + 512)), prs64, ALU.mult,
                    reads=[att_pair.buf, prsb], writes=[attn_t.buf])
                vop(cx, "tensor_scalar", attn_t.ap(ps=(base, base + 64)),
                    attn_t.ap(ps=(base, base + 64)), gm2.ap(ps=(base, base + 64)),
                    bt2.ap(ps=(base, base + 64)), ALU.mult, ALU.add,
                    reads=[attn_t.buf, gm2.buf, bt2.buf], writes=[attn_t.buf])
                gsl = gated.ap(ps=(base, base + 64), cs=(pair * T + cs0, pair * T + cs0 + 512))
                vop(cx, "tensor_tensor", gsl, attn_t.ap(ps=(base, base + 64)),
                    gT.ap(ps=(base, base + 64), cs=(pair * T + cs0, pair * T + cs0 + 512)),
                    ALU.mult, reads=[attn_t.buf, gT.buf], writes=[gated.buf])
    pr.barrier()
    ar.pop()

    ar.push()
    wo_sb = ar.alloc(BF16, 8 * D, name="wo_sb")
    xb = [ar.alloc(F32, 512, name=f"xb{i}") for i in range(2)]
    oo = [ar.alloc(F32, 512, name=f"oo{i}") for i in range(2)]
    dma(cx, "g", wo_sb.ap().rearrange("p (ct o) -> p ct o", ct=8),
        io["wout"].ap().rearrange("(ct p) o -> p ct o", p=128), reads=[iob["wout"]], writes=[wo_sb.buf])
    for qc in range(4):
        cs0 = qc * 512
        for dt in range(8):
            x_t = xb[dt % 2]
            o_t = oo[dt % 2]
            dma(cx, "y", x_t.ap(), io["x1b"][dt * 128:(dt + 1) * 128, cs0:cs0 + 512],
                reads=[iob["x1b"]], writes=[x_t.buf])
            pa, pab = cx.pbank(0 if dt % 2 == 0 else 1)
            for ct in range(8):
                mm(cx, pa, wo_sb.ap(cs=(ct * D + dt * 128, ct * D + (dt + 1) * 128)),
                   gated.ap(cs=(ct * T + cs0, ct * T + cs0 + 512)), pab,
                   [wo_sb.buf, gated.buf], ct == 0, ct == 7)
            vop(cx, "tensor_tensor", o_t.ap(), pa, x_t.ap(), ALU.add,
                reads=[pab, x_t.buf], writes=[o_t.buf])
            dma(cx, "y", io["outT"][dt * 128:(dt + 1) * 128, cs0:cs0 + 512], o_t.ap(),
                reads=[o_t.buf], writes=[iob["outT"]])
    ar.pop()
    pr.barrier()



L3_INS = [("wout", [D, D], "f"), ("gamma2", [128, 1], "f"), ("beta2", [128, 1], "f")]


def build_launch3(debug=False):
    nc = bass.Bass("TRN2", target_bir_lowering=False, debug=False)
    import contextlib
    ctx = contextlib.ExitStack()
    cx = Ctx(nc, ctx, arena_bytes=211500)
    io = {}
    io.update(_declare(nc, [("x1b", [D, T], "f"), ("qTf", [D, T], "b"), ("gTf", [D, T], "b"),
                            ("kTa", [CPB * D, T], "b"), ("vNa", [SQ, D], "b")], "ExternalInput"))
    io.update(_declare(nc, L3_INS, "ExternalInput"))
    io.update(_declare(nc, [("outT", [D, T], "f")], "ExternalOutput"))
    iob = {n: Buf(n) for n in list(io)}
    with ctx:
        _emit_l3(cx, io, iob, debug=debug)
        with nc.Block() as block, _sems(nc, ctx, cx.prog.sems) as sems:
            cx.prog.flush(block, sems, nc=nc)
    return nc



def build_fused(debug=False):
    nc = bass.Bass("TRN2", target_bir_lowering=False, debug=False)
    import contextlib
    ctx = contextlib.ExitStack()
    cx = Ctx(nc, ctx, arena_bytes=211500)
    pr = cx.prog
    GS = G + 2
    io = {}
    io.update(_declare(nc, L1_INS, "ExternalInput"))
    io.update(_declare(nc, L1_MID, None))
    io.update(_declare(nc, L1_PRECAST, None))
    io.update(_declare(nc, L2_INS, "ExternalInput"))
    io.update(_declare(nc, L2_MID, None))
    io.update(_declare(nc, L3_INS, "ExternalInput"))
    io["mprem"] = nc.dram_tensor("mprem", [128, 8], F32, kind="ExternalInput")
    io["outT"] = nc.dram_tensor("outT", [D, T], F32, kind="ExternalOutput")
    io["mtot_ag"] = nc.dram_tensor("mtot_ag", [NCORES * 128, 4 * GS], F32)
    io["kTa"] = nc.dram_tensor("kTa", [CPB * D, T], BF16)
    io["vNa"] = nc.dram_tensor("vNa", [SQ, D], BF16)
    for n in ("xn_dbg", "a0_dbg", "x1_dbg"):
        io[n] = None
    iob = {n: Buf(n) for n in io}

    def init_state(cx2, m_sb, mp):
        ar = cx2.ar
        mask = ar.alloc(F32, 8, name="mprem")
        dma(cx2, "y", mask.ap(), io["mprem"][:, :], reads=[iob["mprem"]], writes=[mask.buf])
        memset(cx2, m_sb.ap(), 0.0, [m_sb.buf])
        for j in range(NCORES):
            mj = mp[j % 2]
            dma(cx2, "y", mj.ap(), io["mtot_ag"][j * 128:(j + 1) * 128, :],
                reads=[iob["mtot_ag"]], writes=[mj.buf])
            vop(cx2, "scalar_tensor_tensor", m_sb.ap(), mj.ap(), mask.ap(cs=(j, j + 1)),
                m_sb.ap(), ALU.mult, ALU.add,
                reads=[mj.buf, mask.buf, m_sb.buf], writes=[m_sb.buf])

    def cc(kind, rg, in_name, out_name):
        def fn(h, k=kind, r=rg, i=io[in_name], o=io[out_name]):
            return h.collective_compute("AllGather", mybir.AluOpType.bypass,
                                        replica_groups=r, ins=[i.ap().opt()],
                                        outs=[o.ap().opt()])
        pr.emit("g", fn, reads=[iob[in_name]], writes=[iob[out_name]], sem="c", step=1)

    with ctx:
        ar = cx.ar
        ar.push()
        _emit_l1(cx, io, iob, debug=False)
        ar.pop()
        pr.barrier()
        cc("AllGather", [list(range(NCORES))], "mtot", "mtot_ag")
        def post_qkuv():
            cc("AllGather", [[0, 1, 2, 3], [4, 5, 6, 7]], "kTf", "kTa")
            cc("AllGather", [[0, 1, 2, 3], [4, 5, 6, 7]], "vNf", "vNa")
        ar.push()
        _emit_l2(cx, io, iob, init_state, debug=False, post_qkuv=post_qkuv)
        ar.pop()
        pr.barrier(skip=("c",))
        ar.push()
        _emit_l3(cx, io, iob, debug=False)
        ar.pop()
        pr.barrier()
        with nc.Block() as block, _sems(nc, ctx, pr.sems) as sems:
            pr.flush(block, sems, nc=nc)
    return nc



def _prep_shared(inputs):
    f = np.ascontiguousarray
    sh = {}
    sh["wq"] = f(inputs["L0_attn_wq"].astype(np.float32))
    sh["wk"] = f(inputs["L0_attn_wk"].astype(np.float32))
    sh["wv"] = f(inputs["L0_attn_wv"].astype(np.float32))
    sh["bq"] = f(inputs["L0_attn_bq"].reshape(4, 128).T.astype(np.float32))
    sh["bk"] = f(inputs["L0_attn_bk"].reshape(4, 128).T.astype(np.float32))
    sh["bk_row"] = f(inputs["L0_attn_bk"].reshape(1, DH).astype(np.float32))
    sh["bv"] = f(inputs["L0_attn_bv"].reshape(1, D).astype(np.float32))
    sh["w1t"] = f(inputs["L0_ffn_w1_w"].T)
    sh["w2t"] = f(inputs["L0_ffn_w2_w"].T)
    sh["h1t"] = f(inputs["L0_ffn_h1_w"].T)
    sh["h2t"] = f(inputs["L0_ffn_h2_w"].T)
    sh["w3t"] = f(inputs["L0_ffn_w3_w"].T)
    for nm, key in (("w1s", "L0_ffn_w1_s"), ("w2s", "L0_ffn_w2_s"),
                    ("h1s", "L0_ffn_h1_s"), ("h2s", "L0_ffn_h2_s"), ("w3s", "L0_ffn_w3_s")):
        sh[nm] = f(inputs[key].reshape(-1, 128).T)
    return sh


def _shard_x(x):
    shards = []
    for c in range(NCORES):
        b, r = c // CPB, (c % CPB) * T
        shards.append(np.ascontiguousarray(x[b, r:r + T, :].T.astype(np.float32)))
    return shards


def run_launch1(inputs, debug=False):
    sh = _prep_shared(inputs)
    xs = _shard_x(np.asarray(inputs["L0_attn_x"]))
    nc = build_launch1(debug=debug)
    in_maps = []
    for c in range(NCORES):
        m = dict(sh)
        m["xT"] = xs[c]
        in_maps.append(m)
    res = run_bass_kernel_spmd(nc, in_maps, core_ids=list(range(NCORES)))
    return res.results


def _prep_l2(inputs):
    f = np.ascontiguousarray
    sh = {}
    sh["tri"] = np.triu(np.ones((128, 128), dtype=np.float32))
    sh["wo"] = f(inputs["L0_attn_wo"].astype(np.float32))
    sh["bo"] = f(inputs["L0_attn_bo"].reshape(8, 128).T.astype(np.float32))
    sh["wquv"] = f(inputs["L1_attn_w_qkuv"].astype(np.float32))
    sh["Wt"] = f(inputs["L1_ffn_W"].T.astype(np.float32))
    sh["Wpt"] = f(inputs["L1_ffn_Wp"].T.astype(np.float32))
    sh["bp"] = f(inputs["L1_ffn_bp"].reshape(2, 128).T.astype(np.float32))
    sh["sim"] = f(inputs["L1_ffn_sim"].astype(np.float32))
    sh["temp"] = f(inputs["L1_ffn_temp"].reshape(1, 1).astype(np.float32))
    wd = np.zeros((128, D), np.float32)
    wu = np.zeros((D, 128), np.float32)
    emat = np.zeros((4, 128), np.float32)
    bases = {3: (0, 8), 2: (32, 4), 1: (64, 2), 0: (96, 1)}
    for e, (b0, r) in bases.items():
        wd[b0:b0 + r, :] = inputs[f"L1_ffn_Wd{e}"]
        wu[:, b0:b0 + r] = inputs[f"L1_ffn_Wu{e}"]
        emat[e, b0:b0 + r] = 1.0
    sh["wdt"] = f(wd.T)
    sh["wut"] = f(wu.T)
    sh["emat"] = emat
    sh["eye"] = np.eye(128, dtype=np.float32)
    return sh


def run_launch2(inputs, l1res, debug=False):
    sh = _prep_l2(inputs)
    nc = build_launch2(debug=debug)
    zeros = np.zeros((128, 4 * 66), np.float32)
    in_maps = []
    for c in range(NCORES):
        m = dict(sh)
        for k in ("px1", "qT", "kT", "kN", "vN"):
            m[k] = np.ascontiguousarray(l1res[c][k])
        preds = [l1res[j]["mtot"] for j in range((c // CPB) * CPB, c)]
        while len(preds) < 3:
            preds.append(zeros)
        m["mpre"] = np.ascontiguousarray(np.stack(preds[:3], 0))
        in_maps.append(m)
    res = run_bass_kernel_spmd(nc, in_maps, core_ids=list(range(NCORES)))
    return res.results


def _prep_l3(inputs):
    f = np.ascontiguousarray
    sh = {}
    sh["wout"] = f(inputs["L1_attn_w_out"].astype(np.float32))
    sh["gamma2"] = f(np.tile(inputs["L1_attn_ln_gamma"].astype(np.float32), 2).reshape(128, 1))
    sh["beta2"] = f(np.tile(inputs["L1_attn_ln_beta"].astype(np.float32), 2).reshape(128, 1))
    return sh


def run_launch3(inputs, l2res, debug=False):
    sh = _prep_l3(inputs)
    nc = build_launch3(debug=debug)
    in_maps = []
    kta = [np.ascontiguousarray(np.concatenate([l2res[b * CPB + r]["kTf"] for r in range(CPB)], 0))
           for b in range(2)]
    vna = [np.ascontiguousarray(np.concatenate([l2res[b * CPB + r]["vNf"] for r in range(CPB)], 0))
           for b in range(2)]
    for c in range(NCORES):
        m = dict(sh)
        m["x1b"] = np.ascontiguousarray(l2res[c]["x1b"])
        m["qTf"] = np.ascontiguousarray(l2res[c]["qTf"])
        m["gTf"] = np.ascontiguousarray(l2res[c]["gTf"])
        m["kTa"] = kta[c // CPB]
        m["vNa"] = vna[c // CPB]
        in_maps.append(m)
    res = run_bass_kernel_spmd(nc, in_maps, core_ids=list(range(NCORES)))
    return res.results


def run_fused(inputs, debug=False):
    sh = {}
    sh.update(_prep_shared(inputs))
    sh.update(_prep_l2(inputs))
    sh.update(_prep_l3(inputs))
    xs = _shard_x(np.asarray(inputs["L0_attn_x"]))
    nc = build_fused(debug=debug)
    in_maps = []
    for c in range(NCORES):
        m = dict(sh)
        m["xT"] = xs[c]
        mask = np.zeros((128, NCORES), np.float32)
        g0 = (c // CPB) * CPB
        mask[:, g0:c] = 1.0
        m["mprem"] = mask
        in_maps.append(m)
    res = run_bass_kernel_spmd(nc, in_maps, core_ids=list(range(NCORES)))
    return res.results


FUSED = True


def kernel(**inputs):
    inputs = {k: np.asarray(v) for k, v in inputs.items()}
    if FUSED:
        res = run_fused(inputs)
        out = np.empty((2, SQ, D), np.float32)
        for c in range(NCORES):
            b, r = c // CPB, (c % CPB) * T
            out[b, r:r + T, :] = res[c]["outT"].T
        return out
    l1 = run_launch1(inputs)
    l2 = run_launch2(inputs, l1)
    l3 = run_launch3(inputs, l2)
    out = np.empty((2, SQ, D), np.float32)
    for c in range(NCORES):
        b, r = c // CPB, (c % CPB) * T
        out[b, r:r + T, :] = l3[c]["outT"].T
    return out
